# version 05
# baseline (speedup 1.0000x reference)
import math
from contextlib import ExitStack

import numpy as np
import ml_dtypes

import concourse.bass as bass
import concourse.mybir as mybir
from concourse.bass_utils import run_bass_kernel_spmd

F32 = mybir.dt.float32
BF16 = mybir.dt.bfloat16
AF = mybir.ActivationFunctionType
ALU = mybir.AluOpType
AX = mybir.AxisListType
BFNP = ml_dtypes.bfloat16

T = 4096
D = 1024
SEG = 512
NEG = -3.0e38
FUSED = False
GROUPED = True
DEBUG_TAPS = set()
PARANOID = False



class Buf:
    __slots__ = ("h", "w", "r")

    def __init__(self, h=None):
        self.h = h
        self.w = None
        self.r = []

    def ap(self):
        return self.h[:]

    def ap_first(self):
        a = self.h[:]
        idx = tuple([slice(0, 1)] * len(a.shape))
        return a[idx]


class Rec:

    ENGS = ("sy", "pe", "dv", "sc", "gp")

    def __init__(self, nc, es):
        self.nc = nc
        self.sem_c = {e: es.enter_context(nc.semaphore(f"smc_{e}")) for e in self.ENGS}
        self.sem_d = {e: es.enter_context(nc.semaphore(f"smd_{e}")) for e in self.ENGS}
        self.cnt_c = {e: 0 for e in self.ENGS}
        self.cnt_d = {e: 0 for e in self.ENGS}
        self.ops = []
        self.waited = {}
        self.fence = []

    def _waits_for(self, eng, bufs_r, bufs_w, extra):
        toks = []
        for b in bufs_r:
            if b.w is not None:
                toks.append(b.w)
        for b in bufs_w:
            if b.w is not None:
                toks.append(b.w)
            toks.extend(b.r)
        if extra:
            toks.extend(extra)
        toks.extend(self.fence)
        need = {}
        for (kind, e, n) in toks:
            if e == eng and kind == "c" and (eng == "pe" or n > self.cnt_c[eng]):
                continue
            key = (kind, e)
            if need.get(key, 0) < n:
                need[key] = n
        return need

    def _record(self, eng, kind, closure, bufs_r, bufs_w, extra, inc):
        need = self._waits_for(eng, bufs_r, bufs_w, extra)
        if PARANOID:
            for e2 in self.ENGS:
                if self.cnt_c[e2] and not (e2 == eng and kind == "c"):
                    k = ("c", e2)
                    need[k] = max(need.get(k, 0), self.cnt_c[e2])
                if self.cnt_d[e2]:
                    k = ("d", e2)
                    need[k] = max(need.get(k, 0), self.cnt_d[e2])
        if inc:
            if kind == "c":
                self.cnt_c[eng] += 1
                tok = ("c", eng, self.cnt_c[eng])
            else:
                self.cnt_d[eng] += 16
                tok = ("d", eng, self.cnt_d[eng])
        else:
            tok = ("c", eng, self.cnt_c[eng] + 1)
        self.ops.append((eng, closure, need, tok if inc else None))
        for b in bufs_r:
            b.r.append(tok)
        for b in bufs_w:
            b.w = tok
            b.r = []
        return tok

    def do(self, eng, meth, *args, reads=(), writes=(), wait=None, inc=True, **kw):
        def closure(E):
            return getattr(E, meth)(*args, **kw)
        return self._record(eng, "c", closure, list(reads), list(writes), wait, inc)

    def dma(self, eng, out, in_, reads=(), writes=(), wait=None, **kw):
        def closure(E):
            return E.dma_start(out=out, in_=in_, **kw)
        return self._record(eng, "d", closure, list(reads), list(writes), wait, True)

    def flush(self):
        if not self.ops:
            return
        ops, self.ops = self.ops, []
        per = {e: [] for e in self.ENGS}
        for op in ops:
            per[op[0]].append(op)
        names = {"sy": "sync", "pe": "tensor", "dv": "vector", "sc": "scalar",
                 "gp": "gpsimd"}
        cm = self.nc.Block()
        blk = cm.__enter__()

        def mk(eng, eng_ops):
            def body(E):
                for (_, closure, need, tok) in eng_ops:
                    for (nk, ne), val in sorted(need.items()):
                        key = (eng, nk, ne)
                        if self.waited.get(key, -1) >= val:
                            continue
                        self.waited[key] = val
                        E.wait_ge(self.sem_c[ne] if nk == "c" else self.sem_d[ne], val)
                    ins = closure(E)
                    if tok is not None:
                        (k2, te, _v) = tok
                        ins.then_inc(self.sem_c[te] if k2 == "c" else self.sem_d[te],
                                     1 if k2 == "c" else 16)
            return body
        for e in self.ENGS:
            if per[e]:
                getattr(blk, names[e])(mk(e, per[e]))
        cm.__exit__(None, None, None)

    def finale(self):
        self.flush()
        cm = self.nc.Block()
        blk = cm.__enter__()
        dd, cc = dict(self.cnt_d), dict(self.cnt_c)
        sd, sc_ = self.sem_d, self.sem_c

        def body(E):
            for e, v in dd.items():
                if v:
                    E.wait_ge(sd[e], v)
            for e, v in cc.items():
                if v and e != "gp":
                    E.wait_ge(sc_[e], v)
        blk.gpsimd(body)
        cm.__exit__(None, None, None)




class Env:
    def __init__(self, nc, es):
        self.nc = nc
        self.es = es
        self.cur = es
        self.r = Rec(nc, es)
        self.dram = {}

    def begin_stage(self):
        assert self.cur is self.es
        self.cur = ExitStack()
        return self.cur

    def end_stage(self):
        self.r.flush()
        self.cur.close()
        self.cur = self.es
        if hasattr(self, "_s0f"):
            del self._s0f
            del self._s0b
        r = self.r
        r.fence = ([("c", e, r.cnt_c[e]) for e in r.ENGS if r.cnt_c[e]] +
                   [("d", e, r.cnt_d[e]) for e in r.ENGS if r.cnt_d[e]])

    def dram_decl(self, name, shape, dtype, kind):
        h = self.nc.dram_tensor(name, list(shape), dtype, kind=kind)
        b = Buf(h)
        self.dram[name] = b
        return b

    def tap(self, key, ap, bufs):
        if not DEBUG_TAPS:
            return
        if key not in DEBUG_TAPS:
            return
        sh = list(ap.shape)
        if ap.space == bass.MemorySpace.SBUF:
            b = self.dram_decl("dbg_" + key, sh, ap.dtype, "ExternalOutput")
            self.r.dma("gp", b.ap(), ap, reads=list(bufs), writes=[b])
        else:
            b = self.dram_decl("dbg_" + key, sh, F32, "ExternalOutput")
            scr = self.sb("dbgscr_" + key, sh, F32)
            self.r.do("dv", "tensor_copy", scr.ap(), ap, reads=list(bufs),
                      writes=[scr])
            self.r.dma("gp", b.ap(), scr.ap(), reads=[scr], writes=[b])

    def sb(self, name, shape, dtype=F32):
        return Buf(self.cur.enter_context(self.nc.sbuf_tensor(name, list(shape), dtype)))

    def ps(self, name, shape, dtype=F32):
        return Buf(self.cur.enter_context(self.nc.psum_tensor(name, list(shape), dtype)))


class Common:
    pass


def setup_common(env, consts):
    r = env.r
    c = Common()
    c.id_f = env.sb("c_idf", [128, 128], F32)
    c.id_b = env.sb("c_idb", [128, 128], BF16)
    c.tri_add = env.sb("c_tri", [128, 128], F32)
    c.ones1 = env.sb("c_ones1", [1, 128], F32)
    c.eps5 = env.sb("c_eps5", [128, 1], F32)
    c.eps6 = env.sb("c_eps6", [128, 1], F32)
    c.scr1 = env.sb("c_scr1", [128, 1024], F32)
    c.scr2 = env.sb("c_scr2", [128, 1024], F32)
    c.scr3 = env.sb("c_scr3", [128, 1024], F32)
    c.bnst = env.sb("c_bnst", [128, 12], F32)
    c.mv = env.sb("c_mv", [128, 2], F32)
    c.mva = env.sb("c_mva", [128, 2], F32)
    c.mvb = env.sb("c_mvb", [128, 2], F32)
    c.dmv = env.sb("c_dmv", [128, 1], F32)
    c.rstd = env.sb("c_rstd", [128, 1], F32)
    c.tmp1 = env.sb("c_tmp1", [128, 1], F32)
    c.tmp2 = env.sb("c_tmp2", [1, 1], F32)
    nc = env.nc
    idf = nc.inline_tensor(np.eye(128, dtype=np.float32), "inl_idf")
    idb = nc.inline_tensor(np.eye(128, dtype=BFNP), "inl_idb")
    tri = np.where(np.arange(128)[None, :] <= np.arange(128)[:, None],
                   np.float32(0.0), np.float32(NEG))
    trit = nc.inline_tensor(tri.astype(np.float32), "inl_tri")
    r.dma("sy", c.id_f.ap(), idf.ap()[:], writes=[c.id_f])
    r.dma("sy", c.id_b.ap(), idb.ap()[:], writes=[c.id_b])
    r.dma("sy", c.tri_add.ap(), trit.ap()[:], writes=[c.tri_add])
    r.do("dv", "memset", c.ones1.ap(), 1.0, writes=[c.ones1])
    r.do("dv", "memset", c.eps5.ap(), 1e-5, writes=[c.eps5])
    r.do("dv", "memset", c.eps6.ap(), 1e-6, writes=[c.eps6])
    return c


def emit_ln(env, c, x_ap, xb, out_ap, outb, eps, subtract_mean=True, post=None):
    r = env.r
    Dw = x_ap.shape[-1]
    nh = Dw // 512
    assert nh in (1, 2)
    for i in range(nh):
        r.do("dv", "bn_stats", c.bnst.ap()[:, 6 * i:6 * (i + 1)],
             x_ap[:, 512 * i:512 * (i + 1)], reads=[xb], writes=[c.bnst])
    if nh == 1:
        r.do("dv", "bn_aggr", c.mv.ap(), c.bnst.ap()[:, 0:6], reads=[c.bnst],
             writes=[c.mv])
    else:
        r.do("dv", "bn_aggr", c.mva.ap(), c.bnst.ap()[:, 0:6], reads=[c.bnst],
             writes=[c.mva])
        r.do("dv", "bn_aggr", c.mvb.ap(), c.bnst.ap()[:, 6:12], reads=[c.bnst],
             writes=[c.mvb])
        r.do("dv", "tensor_add", c.mv.ap(), c.mva.ap(), c.mvb.ap(),
             reads=[c.mva, c.mvb], writes=[c.mv])
        r.do("dv", "tensor_scalar_mul", c.mv.ap(), c.mv.ap(), 0.5,
             reads=[c.mv], writes=[c.mv])
        r.do("dv", "tensor_sub", c.dmv.ap(), c.mva.ap()[:, 0:1],
             c.mvb.ap()[:, 0:1], reads=[c.mva, c.mvb], writes=[c.dmv])
        r.do("dv", "tensor_mul", c.dmv.ap(), c.dmv.ap(), c.dmv.ap(),
             reads=[c.dmv], writes=[c.dmv])
        r.do("dv", "scalar_tensor_tensor", c.mv.ap()[:, 1:2], c.dmv.ap(), 0.25,
             c.mv.ap()[:, 1:2], ALU.mult, ALU.add, reads=[c.dmv, c.mv],
             writes=[c.mv])
    if subtract_mean:
        r.do("sc", "activation", c.tmp1.ap(), c.mv.ap()[:, 1:2], func=AF.Ln,
             bias=eps.ap(), reads=[c.mv, eps], writes=[c.tmp1])
    else:
        r.do("dv", "tensor_tensor", c.tmp1.ap(), c.mv.ap()[:, 0:1],
             c.mv.ap()[:, 0:1], ALU.mult, reads=[c.mv], writes=[c.tmp1])
        r.do("dv", "tensor_add", c.tmp1.ap(), c.tmp1.ap(), c.mv.ap()[:, 1:2],
             reads=[c.mv, c.tmp1], writes=[c.tmp1])
        r.do("sc", "activation", c.tmp1.ap(), c.tmp1.ap(), func=AF.Ln,
             bias=eps.ap(), reads=[c.tmp1, eps], writes=[c.tmp1])
    r.do("sc", "activation", c.rstd.ap(), c.tmp1.ap(), func=AF.Exp, scale=-0.5,
         reads=[c.tmp1], writes=[c.rstd])
    if post is None:
        if subtract_mean:
            r.do("dv", "tensor_scalar", out_ap, x_ap, c.mv.ap()[:, 0:1],
                 c.rstd.ap(), ALU.subtract, ALU.mult,
                 reads=[xb, c.mv, c.rstd], writes=[outb])
        else:
            r.do("dv", "tensor_scalar", out_ap, x_ap, c.rstd.ap(), None,
                 ALU.mult, reads=[xb, c.rstd], writes=[outb])
    else:
        kind, wbc = post
        assert kind == "mulw" and not subtract_mean
        r.do("dv", "scalar_tensor_tensor", out_ap, x_ap, c.rstd.ap(),
             wbc.ap()[:, 0:Dw], ALU.mult, ALU.mult,
             reads=[xb, c.rstd, wbc], writes=[outb])


def sc_accum_fence(env, c, buf):
    env.r.do("sc", "copy", c.tmp2.ap(), buf.ap_first(), reads=[], writes=[buf, c.tmp2])


def emit_bcast_row(env, c, src_row_ap, src_buf, dst_psum_buf, width):
    env.r.do("pe", "matmul", dst_psum_buf.ap()[:, 0:width], c.ones1.ap(),
             src_row_ap, start=True, stop=True,
             reads=[c.ones1, src_buf], writes=[dst_psum_buf])


ROPE_OPS = 6


def emit_rope(env, c, x_ap, xbufs, out_ap, outb, cs, csb, nh, dh):
    r = env.r
    lim = ROPE_OPS
    k = dh // 2
    xv = x_ap.rearrange("p (h k two) -> p h k two", k=k, two=2)
    ov = out_ap.rearrange("p (h k two) -> p h k two", k=k, two=2)
    s1 = c.scr1.ap()[:, 0:nh * dh].rearrange("p (h k two) -> p h k two", k=k, two=2)
    s2 = c.scr2.ap()[:, 0:nh * dh].rearrange("p (h k two) -> p h k two", k=k, two=2)
    cos = cs[:, 0:k].unsqueeze(1).broadcast_to([128, nh, k])
    sin = cs[:, k:dh].unsqueeze(1).broadcast_to([128, nh, k])
    rd = list(xbufs) + [csb]
    (lim > 0) and r.do("dv", "tensor_tensor", s1[..., 0], xv[..., 0], cos, ALU.mult,
             reads=rd, writes=[c.scr1])
    (lim > 1) and r.do("dv", "tensor_tensor", s1[..., 1], xv[..., 1], cos, ALU.mult,
             reads=rd, writes=[c.scr1])
    (lim > 2) and r.do("dv", "tensor_tensor", s2[..., 0], xv[..., 1], sin, ALU.mult,
             reads=rd, writes=[c.scr2])
    (lim > 3) and r.do("dv", "tensor_tensor", s2[..., 1], xv[..., 0], sin, ALU.mult,
             reads=rd, writes=[c.scr2])
    (lim > 4) and r.do("dv", "tensor_sub", ov[..., 0], s1[..., 0], s2[..., 0],
             reads=[c.scr1, c.scr2], writes=[outb])
    (lim > 5) and r.do("dv", "tensor_add", ov[..., 1], s1[..., 1], s2[..., 1],
             reads=[c.scr1, c.scr2], writes=[outb])



def emit_bf16_cache(env, name, row0=0, rows=None, dst_sub=None):
    r = env.r
    if not hasattr(env, "_s0f"):
        env._s0f = [env.sb(f"s0f{env.r.cnt_c['gp']}_{i}", [128, 4096], F32)
                    for i in range(2)]
        env._s0b = [env.sb(f"s0b{env.r.cnt_c['gp']}_{i}", [128, 4096], BF16)
                    for i in range(2)]
    src_buf = env.dram[name]
    dst_buf = dst_sub if dst_sub is not None else env.dram[name + "_b"]
    allr, cols = dict(WEIGHT_SPECS)[name]
    if rows is None:
        rows = allr
    sv = src_buf.h[:].rearrange("(n p) c -> n p c", p=128)
    dv = env.dram[name + "_b"].h[:].rearrange("(n p) c -> n p c", p=128)
    i = 0
    for bi in range(row0 // 128, (row0 + rows) // 128):
        for c0 in range(0, cols, 4096):
            w = min(4096, cols - c0)
            f, b = env._s0f[i % 2], env._s0b[i % 2]
            r.dma("sy", f.ap()[:, 0:w], sv[bi, :, c0:c0 + w],
                  reads=[src_buf], writes=[f])
            r.do("gp", "tensor_copy", b.ap()[:, 0:w], f.ap()[:, 0:w],
                 reads=[f], writes=[b])
            r.dma("gp", dv[bi, :, c0:c0 + w], b.ap()[:, 0:w],
                  reads=[b], writes=[dst_buf])
            i += 1



def host_prep(I):
    H = {}
    xf = np.ascontiguousarray(I["L0_attn_seq"])
    shards = []
    for c in range(8):
        b, rr = c // 4, c % 4
        segs = [xf[b, (rr + 4 * j) * SEG:(rr + 4 * j + 1) * SEG] for j in range(8)]
        shards.append(np.ascontiguousarray(np.concatenate(segs, axis=0)))
    H["x0"] = shards
    tr = lambda a: np.ascontiguousarray(np.transpose(a))
    H["w_qkv"] = np.ascontiguousarray(I["L0_attn_w_qkv"])
    H["w_out"] = np.ascontiguousarray(I["L0_attn_w_out"])
    H["b_out"] = np.ascontiguousarray(I["L0_attn_b_out"]).reshape(1, D)
    H["norm_w"] = np.ascontiguousarray(I["L0_attn_norm_w"]).reshape(1, D)
    H["pm_kT"] = np.ascontiguousarray(np.transpose(I["L0_attn_pm_k"], (1, 2, 0)))
    H["pm_kT"] = np.ascontiguousarray(
        np.concatenate([I["L0_attn_pm_k"][h].T for h in range(16)], axis=1))
    H["pm_v"] = np.ascontiguousarray(I["L0_attn_pm_v"])
    H["WqT"] = tr(I["L0_ffn_Wq"])
    H["WkcT"] = tr(I["L0_ffn_Wkc"])
    H["WvcT"] = tr(I["L0_ffn_Wvc"])
    H["WoT0"] = tr(I["L0_ffn_Wo"])
    H["WrT"] = tr(I["L0_ffn_Wr"])
    H["ebias"] = np.ascontiguousarray(I["L0_ffn_expert_bias"]).reshape(1, 8)
    H["g1"] = np.ascontiguousarray(I["L0_ffn_g1"]).reshape(1, D)
    H["g2"] = np.ascontiguousarray(I["L0_ffn_g2"]).reshape(1, D)
    H["sW1T"] = tr(I["L0_ffn_sW1"])
    H["sW3T"] = tr(I["L0_ffn_sW3"])
    H["sW2T"] = tr(I["L0_ffn_sW2"])
    H["eW1T"] = np.ascontiguousarray(np.stack([I["L0_ffn_eW1"][e].T for e in range(8)]))
    H["eW3T"] = np.ascontiguousarray(np.stack([I["L0_ffn_eW3"][e].T for e in range(8)]))
    H["eW2T"] = np.ascontiguousarray(np.stack([I["L0_ffn_eW2"][e].T for e in range(8)]))
    H["Wcat1"] = np.ascontiguousarray(np.concatenate(
        [I["L1_attn_Wq"][0].T, I["L1_attn_Wq"][1].T,
         I["L1_attn_Wk"].T, I["L1_attn_Wv"].T], axis=1))
    H["WoT1"] = tr(I["L1_attn_Wo"])
    H["rW"] = np.ascontiguousarray(I["L1_ffn_router_W"])
    H["rb"] = np.ascontiguousarray(I["L1_ffn_router_b"]).reshape(1, 8)
    H["ln_gT"] = tr(I["L1_ffn_ln_g"])
    H["ln_bT"] = tr(I["L1_ffn_ln_b"])
    H["W1_1"] = np.ascontiguousarray(I["L1_ffn_W1"])
    H["b1T"] = tr(I["L1_ffn_b1"])
    H["W2_1"] = np.ascontiguousarray(I["L1_ffn_W2"])
    H["b2_1"] = np.ascontiguousarray(I["L1_ffn_b2"])
    cs0, cs1q, cs1k = [], [], []
    for c in range(8):
        rr = c % 4
        pos = np.concatenate([np.arange((rr + 4 * j) * SEG, (rr + 4 * j + 1) * SEG)
                              for j in range(8)]).astype(np.float64)
        inv0 = 1.0 / 10000.0 ** (np.arange(0, 64, 2, dtype=np.float64) / 64)
        f0 = pos[:, None] * inv0
        t0 = np.concatenate([np.cos(f0), np.sin(f0)], axis=1).astype(np.float32)
        cs0.append(np.ascontiguousarray(t0))
        inv1 = 1.0 / 10000.0 ** (2.0 * np.arange(0, 128, 2, dtype=np.float64) / 128)
        f1 = pos[:, None] * inv1
        sc = 128.0 ** -0.5
        t1q = np.concatenate([np.cos(f1) * sc, np.sin(f1) * sc], axis=1).astype(np.float32)
        t1k = np.concatenate([np.cos(f1), np.sin(f1)], axis=1).astype(np.float32)
        cs1q.append(np.ascontiguousarray(t1q))
        cs1k.append(np.ascontiguousarray(t1k))
    H["cs0"] = cs0
    H["cs1q"] = cs1q
    H["cs1k"] = cs1k
    return H


WEIGHT_SPECS = [
    ("w_qkv", (1024, 3072)),
    ("w_out", (1024, 1024)),
    ("WqT", (1024, 1024)),
    ("WkcT", (1024, 256)),
    ("WvcT", (1024, 256)),
    ("WoT0", (1024, 1024)),
    ("sW1T", (1024, 1024)),
    ("sW3T", (1024, 1024)),
    ("sW2T", (1024, 1024)),
    ("eW1T", (8 * 1024, 1024)),
    ("eW3T", (8 * 1024, 1024)),
    ("eW2T", (8 * 1024, 1024)),
    ("Wcat1", (1024, 512)),
    ("WoT1", (256, 1024)),
    ("W1_1", (8 * 1024, 4096)),
    ("W2_1", (8 * 4096, 1024)),
]


SMALL_INPUTS = [("b_out", (1, D)), ("norm_w", (1, D)), ("pm_kT", (64, 256)),
                ("pm_v", (16, 16, 64)), ("WrT", (1024, 8)), ("ebias", (1, 8)),
                ("g1", (1, D)), ("g2", (1, D)), ("rW", (1024, 8)), ("rb", (1, 8)),
                ("ln_gT", (1024, 8)), ("ln_bT", (1024, 8)), ("b1T", (4096, 8)),
                ("b2_1", (8, 1024)), ("cs0", (T, 64)), ("cs1q", (T, 128)),
                ("cs1k", (T, 128)), ("amaskA", (16, 128, 512)),
                ("amaskB", (16, 128, 512))]

HANDOFF = [("X", (T, D), F32), ("X2", (T, D), F32), ("X2b", (T, D), F32),
           ("Xb", (T, D), F32), ("Xc", (T, D), F32), ("QL0", (1024, T), BF16),
           ("KC_loc", (256, T), BF16), ("VC_loc", (T, 256), BF16),
           ("KCG", (1024, T), BF16), ("VCG", (4 * T, 256), BF16),
           ("QL1", (256, T), BF16), ("KL_loc", (128, T), BF16),
           ("VL_loc", (T, 128), BF16), ("KG", (512, T), BF16),
           ("VG", (4 * T, 128), BF16)]


def declare_io(env, ins=(), outs=(), weights=(), smalls=(), final=False,
               need_x0=False):
    e = env
    hs = {nm: (sh, dt) for (nm, sh, dt) in HANDOFF}
    if need_x0:
        e.dram_decl("x0", (T, D), F32, "ExternalInput")
    for nm in weights:
        sh = dict(WEIGHT_SPECS)[nm]
        e.dram_decl(nm, sh, F32, "ExternalInput")
        e.dram_decl(nm + "_b", sh, BF16, "Internal")
    for nm in smalls:
        sh = dict(SMALL_INPUTS)[nm]
        e.dram_decl(nm, sh, F32, "ExternalInput")
    for nm in ins:
        sh, dt = hs[nm]
        e.dram_decl(nm, sh, dt, "ExternalInput")
    for nm in outs:
        sh, dt = hs[nm]
        e.dram_decl(nm, sh, dt, "ExternalOutput")
    if final:
        e.dram_decl("Y", (T, D), F32, "ExternalOutput")


def declare_io_fused(env):
    e = env
    e.dram_decl("x0", (T, D), F32, "ExternalInput")
    for (nm, sh) in WEIGHT_SPECS:
        e.dram_decl(nm, sh, F32, "ExternalInput")
        e.dram_decl(nm + "_b", sh, BF16, "Internal")
    for nm, sh in SMALL_INPUTS:
        e.dram_decl(nm, sh, F32, "ExternalInput")
    for (nm, sh, dt) in HANDOFF:
        e.dram_decl(nm, sh, dt, "Internal")
    e.dram_decl("Y", (T, D), F32, "ExternalOutput")


def emit_S0(env, names):
    for nm in names:
        emit_bf16_cache(env, nm)



def emit_S1(env, c):
    r = env.r
    nc = env.nc
    es = env.es
    dr = env.dram
    wqkv = env.sb("s1_wqkv", [128, 8, 3072], BF16)
    wout = env.sb("s1_wout", [128, 8, 1024], BF16)
    r.dma("sy", wqkv.ap(), dr["w_qkv_b"].h[:].rearrange("(a p) f -> p a f", p=128),
          reads=[dr["w_qkv_b"]], writes=[wqkv])
    r.dma("sy", wout.ap(), dr["w_out_b"].h[:].rearrange("(a p) f -> p a f", p=128),
          reads=[dr["w_out_b"]], writes=[wout])
    nwrow = env.sb("s1_nwrow", [1, D], F32)
    borow = env.sb("s1_borow", [1, D], F32)
    r.dma("sy", nwrow.ap(), dr["norm_w"].h[:], reads=[dr["norm_w"]], writes=[nwrow])
    r.dma("sy", borow.ap(), dr["b_out"].h[:], reads=[dr["b_out"]], writes=[borow])
    nwbc = env.sb("s1_nwbc", [128, D], F32)
    bobc = env.sb("s1_bobc", [128, D], F32)
    pbc = env.ps("s1_pbc", [128, 512], F32)
    for (row, rowb, dst) in ((nwrow.ap(), nwrow, nwbc), (borow.ap(), borow, bobc)):
        for hh in range(2):
            r.do("pe", "matmul", pbc.ap(), c.ones1.ap(), row[:, 512 * hh:512 * (hh + 1)],
                 start=True, stop=True, reads=[c.ones1, rowb], writes=[pbc])
            r.do("sc", "copy", dst.ap()[:, 512 * hh:512 * (hh + 1)], pbc.ap(),
                 reads=[pbc], writes=[dst])
    pmk_f = env.sb("s1_pmkf", [64, 256], F32)
    pmk = env.sb("s1_pmk", [64, 256], BF16)
    r.dma("sy", pmk_f.ap(), dr["pm_kT"].h[:], reads=[dr["pm_kT"]], writes=[pmk_f])
    r.do("gp", "tensor_copy", pmk.ap(), pmk_f.ap(), reads=[pmk_f], writes=[pmk])
    pmv_f = env.sb("s1_pmvf", [16, 16, 64], F32)
    pmv = env.sb("s1_pmv", [16, 16, 65], BF16)
    r.dma("sy", pmv_f.ap(), dr["pm_v"].h[:], reads=[dr["pm_v"]], writes=[pmv_f])
    r.do("gp", "tensor_copy", pmv.ap()[:, :, 0:64], pmv_f.ap(), reads=[pmv_f],
         writes=[pmv])
    r.do("gp", "memset", pmv.ap()[:, :, 64:65], 1.0, writes=[pmv])
    xseg = env.sb("s1_xseg", [128, 4, D], F32)
    lnt = env.sb("s1_lnt", [128, D], F32)
    xn = env.sb("s1_xn", [128, D], BF16)
    xn_fm = env.sb("s1_xnfm", [128, 8, 512], BF16)
    q_sb = env.sb("s1_q", [128, 4, 1024], BF16)
    k_sb = env.sb("s1_k", [128, 4, 1024], BF16)
    v_sb = env.sb("s1_v", [128, 4, 16, 65], BF16)
    cs_t = env.sb("s1_cs", [128, 64], F32)
    qfm = env.sb("s1_qfm", [64, 512], BF16)
    kfm = env.sb("s1_kfm", [64, 512], BF16)
    pA_out = env.sb("s1_pA", [128, 4, 544], BF16)
    dmsk = env.sb("s1_dmsk", [128, 128], F32)
    negm = env.sb("s1_negm", [128, 4], F32)
    lsum = env.sb("s1_lsum", [128, 4, 2], F32)
    pt_sb = env.sb("s1_pt", [128, 4, 512], BF16)
    ptpm = env.sb("s1_ptpm", [16, 512], BF16)
    reciprow = env.sb("s1_rrow", [1, 512], F32)
    lbc_sb = env.sb("s1_lbc", [64, 512], F32)
    attn_fm = env.sb("s1_attnfm", [128, 8, 512], BF16)
    proj = env.sb("s1_proj", [128, D], F32)
    xout = env.sb("s1_xout", [128, D], F32)
    pqk = env.ps("s1_pqk", [128, 512], F32)
    psc = env.ps("s1_psc", [128, 512], F32)
    ppm = env.ps("s1_ppm", [128, 16], F32)
    ptr = env.ps("s1_ptr", [128, 128], BF16)
    ptr2 = env.ps("s1_ptr2", [128, 128], BF16)
    ppv = env.ps("s1_ppv", [65, 512], F32)
    plb = env.ps("s1_plb", [64, 512], F32)

    x0v = dr["x0"].h[:].rearrange("(s q p) d -> s q p d", q=4, p=128)
    Xv = dr["X"].h[:].rearrange("(s q p) d -> s q p d", q=4, p=128)
    cs0v = dr["cs0"].h[:].rearrange("(s q p) d -> s q p d", q=4, p=128)

    for s in range(8):
        for qi in range(4):
            r.dma("sy", xseg.ap()[:, qi], x0v[s, qi], reads=[dr["x0"]],
                  writes=[xseg])
        if s == 0:
            env.tap("xseg0", xseg.ap(), [xseg])
        for qi in range(4):
            emit_ln(env, c, xseg.ap()[:, qi], xseg, lnt.ap(), lnt, c.eps5, True)
            if s == 0 and qi == 0:
                env.tap("ln0", lnt.ap(), [lnt])
                env.tap("nwbc", nwbc.ap(), [nwbc])
            emit_ln(env, c, lnt.ap(), lnt, xn.ap(), xn, c.eps6, False,
                    post=("mulw", nwbc))
            if s == 0 and qi == 0:
                env.tap("xn0", xn.ap(), [xn])
            for a in range(8):
                tp = ptr if a % 2 == 0 else ptr2
                r.do("pe", "transpose", tp.ap(), xn.ap()[:, a * 128:(a + 1) * 128],
                     c.id_b.ap(), reads=[xn, c.id_b], writes=[tp])
                r.do("sc", "copy", xn_fm.ap()[:, a, qi * 128:(qi + 1) * 128],
                     tp.ap(), reads=[tp], writes=[xn_fm])
            r.dma("sy", cs_t.ap(), cs0v[s, qi], reads=[dr["cs0"]], writes=[cs_t])
            for blk in range(6):
                for a in range(8):
                    r.do("pe", "matmul", pqk.ap(),
                         xn_fm.ap()[:, a, qi * 128:(qi + 1) * 128],
                         wqkv.ap()[:, a, blk * 512:(blk + 1) * 512],
                         start=(a == 0), stop=(a == 7),
                         reads=[xn_fm, wqkv], writes=[pqk])
                if blk < 2:
                    r.do("sc", "copy", c.scr3.ap()[:, 0:512], pqk.ap(),
                         reads=[pqk], writes=[c.scr3])
                    emit_rope(env, c, c.scr3.ap()[:, 0:512], [c.scr3],
                              q_sb.ap()[:, qi, blk * 512:(blk + 1) * 512], q_sb,
                              cs_t.ap(), cs_t, 8, 64)
                elif blk < 4:
                    r.do("sc", "copy", c.scr3.ap()[:, 0:512], pqk.ap(),
                         reads=[pqk], writes=[c.scr3])
                    emit_rope(env, c, c.scr3.ap()[:, 0:512], [c.scr3],
                              k_sb.ap()[:, qi, (blk - 2) * 512:(blk - 1) * 512], k_sb,
                              cs_t.ap(), cs_t, 8, 64)
                else:
                    hv = pqk.ap().rearrange("p (h e) -> p h e", e=64)
                    r.do("sc", "copy",
                         v_sb.ap()[:, qi, (blk - 4) * 8:(blk - 3) * 8, 0:64],
                         hv, reads=[pqk], writes=[v_sb])
        r.do("dv", "memset", v_sb.ap()[:, :, :, 64:65], 1.0, writes=[v_sb])
        r.do("gp", "tensor_scalar_mul", q_sb.ap(), q_sb.ap(), 0.125,
             reads=[q_sb], writes=[q_sb])
        if s == 0:
            env.tap("xn_fm", xn_fm.ap(), [xn_fm])
            env.tap("q_sb", q_sb.ap(), [q_sb])
            env.tap("k_sb", k_sb.ap(), [k_sb])
            env.tap("v_sb", v_sb.ap(), [v_sb])
        for h in range(16):
            for qi in range(4):
                tp = ptr if qi % 2 == 0 else ptr2
                r.do("pe", "transpose", tp.ap()[0:64, :],
                     q_sb.ap()[:, qi, h * 64:(h + 1) * 64], c.id_b.ap(),
                     reads=[q_sb, c.id_b], writes=[tp])
                r.do("sc", "copy", qfm.ap()[:, qi * 128:(qi + 1) * 128],
                     tp.ap()[0:64, :], reads=[tp], writes=[qfm])
                r.do("pe", "transpose", tp.ap()[0:64, :],
                     k_sb.ap()[:, qi, h * 64:(h + 1) * 64], c.id_b.ap(),
                     reads=[k_sb, c.id_b], writes=[tp])
                r.do("sc", "copy", kfm.ap()[:, qi * 128:(qi + 1) * 128],
                     tp.ap()[0:64, :], reads=[tp], writes=[kfm])
            for qi in range(4):
                w = (qi + 1) * 128
                r.do("pe", "matmul", ppm.ap(), qfm.ap()[:, qi * 128:(qi + 1) * 128],
                     pmk.ap()[:, h * 16:(h + 1) * 16], start=True, stop=True,
                     reads=[qfm, pmk], writes=[ppm])
                r.do("pe", "matmul", psc.ap()[:, 0:w],
                     qfm.ap()[:, qi * 128:(qi + 1) * 128], kfm.ap()[:, 0:w],
                     start=True, stop=True, reads=[qfm, kfm], writes=[psc])
                r.do("dv", "tensor_add", dmsk.ap(), psc.ap()[:, w - 128:w],
                     c.tri_add.ap(), reads=[psc, c.tri_add], writes=[dmsk])
                r.do("dv", "reduce_max", negm.ap()[:, qi:qi + 1], ppm.ap(),
                     axis=AX.X, negate=True, reads=[ppm], writes=[negm])
                r.do("dv", "reduce_max", c.tmp1.ap(), dmsk.ap(),
                     axis=AX.X, negate=True, reads=[dmsk], writes=[c.tmp1])
                r.do("dv", "tensor_tensor", negm.ap()[:, qi:qi + 1],
                     negm.ap()[:, qi:qi + 1], c.tmp1.ap(), ALU.min,
                     reads=[c.tmp1, negm], writes=[negm])
                if w > 128:
                    r.do("dv", "reduce_max", c.tmp1.ap(), psc.ap()[:, 0:w - 128],
                         axis=AX.X, negate=True, reads=[psc], writes=[c.tmp1])
                    r.do("dv", "tensor_tensor", negm.ap()[:, qi:qi + 1],
                         negm.ap()[:, qi:qi + 1], c.tmp1.ap(), ALU.min,
                         reads=[c.tmp1, negm], writes=[negm])
                r.do("sc", "activation", pA_out.ap()[:, qi, 0:16], ppm.ap(),
                     func=AF.Exp, bias=negm.ap()[:, qi:qi + 1],
                     reads=[ppm, negm], writes=[pA_out])
                r.do("sc", "activation", pA_out.ap()[:, qi, 16 + w - 128:16 + w],
                     dmsk.ap(), func=AF.Exp, bias=negm.ap()[:, qi:qi + 1],
                     reads=[dmsk, negm], writes=[pA_out])
                if w > 128:
                    r.do("sc", "activation", pA_out.ap()[:, qi, 16:16 + w - 128],
                         psc.ap()[:, 0:w - 128], func=AF.Exp,
                         bias=negm.ap()[:, qi:qi + 1],
                         reads=[psc, negm], writes=[pA_out])
            r.do("dv", "memset", pt_sb.ap(), 0.0, writes=[pt_sb])
            r.do("dv", "memset", ptpm.ap(), 0.0, writes=[ptpm])
            for j in range(4):
                for qi in range(j, 4):
                    tp = ptr if (j + qi) % 2 == 0 else ptr2
                    r.do("pe", "transpose", tp.ap(),
                         pA_out.ap()[:, qi, 16 + j * 128:16 + (j + 1) * 128],
                         c.id_b.ap(), reads=[pA_out, c.id_b], writes=[tp])
                    r.do("sc", "copy", pt_sb.ap()[:, j, qi * 128:(qi + 1) * 128],
                         tp.ap(), reads=[tp], writes=[pt_sb])
            for qi in range(4):
                tp = ptr if qi % 2 == 0 else ptr2
                r.do("pe", "transpose", tp.ap()[0:16, :],
                     pA_out.ap()[:, qi, 0:16], c.id_b.ap(),
                     reads=[pA_out, c.id_b], writes=[tp])
                r.do("sc", "copy", ptpm.ap()[:, qi * 128:(qi + 1) * 128],
                     tp.ap()[0:16, :], reads=[tp], writes=[ptpm])
            r.do("pe", "matmul", ppv.ap(), pmv.ap()[:, h, :], ptpm.ap(),
                 start=True, stop=False, reads=[pmv, ptpm], writes=[ppv])
            for j in range(4):
                r.do("pe", "matmul", ppv.ap(), v_sb.ap()[:, j, h, :],
                     pt_sb.ap()[:, j, :], start=False, stop=(j == 3),
                     reads=[v_sb, pt_sb], writes=[ppv])
            r.do("dv", "reciprocal", reciprow.ap(), ppv.ap()[64:65, :],
                 reads=[ppv], writes=[reciprow])
            r.do("pe", "matmul", plb.ap(), c.ones1.ap()[:, 0:64], reciprow.ap(),
                 start=True, stop=True, reads=[c.ones1, reciprow], writes=[plb])
            r.do("sc", "copy", lbc_sb.ap(), plb.ap(), reads=[plb], writes=[lbc_sb])
            r.do("dv", "tensor_tensor",
                 attn_fm.ap()[(h % 2) * 64:(h % 2) * 64 + 64, h // 2, :],
                 ppv.ap()[0:64, :], lbc_sb.ap(), ALU.mult,
                 reads=[ppv, lbc_sb], writes=[attn_fm])
            if s == 0 and h == 0:
                env.tap("pA0", pA_out.ap(), [pA_out])
                env.tap("pt0", pt_sb.ap(), [pt_sb])
                env.tap("qfm0", qfm.ap(), [qfm])
                env.tap("kfm0", kfm.ap(), [kfm])
                env.tap("ppv0", ppv.ap(), [ppv])
        for qi in range(4):
            for half in range(2):
                pw = pqk if half == 0 else psc
                for a in range(8):
                    r.do("pe", "matmul", pw.ap(),
                         attn_fm.ap()[:, a, qi * 128:(qi + 1) * 128],
                         wout.ap()[:, a, half * 512:(half + 1) * 512],
                         start=(a == 0), stop=(a == 7),
                         reads=[attn_fm, wout], writes=[pw])
                r.do("dv", "tensor_add", proj.ap()[:, half * 512:(half + 1) * 512],
                     pw.ap(), bobc.ap()[:, half * 512:(half + 1) * 512],
                     reads=[pw, bobc], writes=[proj])
            if s == 0 and qi == 0:
                env.tap("proj0", proj.ap(), [proj])
            emit_ln(env, c, proj.ap(), proj, lnt.ap(), lnt, c.eps5, True)
            r.do("dv", "tensor_add", xout.ap(), xseg.ap()[:, qi], lnt.ap(),
                 reads=[xseg, lnt], writes=[xout])
            r.dma("gp", Xv[s, qi], xout.ap(), reads=[xout], writes=[dr["X"]])





def emit_S2(env, c):
    r = env.r
    dr = env.dram
    wq = env.sb("s2_wq", [128, 8, 1024], BF16)
    wkc = env.sb("s2_wkc", [128, 8, 256], BF16)
    wvc = env.sb("s2_wvc", [128, 8, 256], BF16)
    r.dma("sy", wq.ap(), dr["WqT_b"].h[:].rearrange("(a p) f -> p a f", p=128),
          reads=[dr["WqT_b"]], writes=[wq])
    r.dma("sy", wkc.ap(), dr["WkcT_b"].h[:].rearrange("(a p) f -> p a f", p=128),
          reads=[dr["WkcT_b"]], writes=[wkc])
    r.dma("sy", wvc.ap(), dr["WvcT_b"].h[:].rearrange("(a p) f -> p a f", p=128),
          reads=[dr["WvcT_b"]], writes=[wvc])
    g1row = env.sb("s2_g1row", [1, D], F32)
    g1bc = env.sb("s2_g1bc", [128, D], F32)
    r.dma("sy", g1row.ap(), dr["g1"].h[:], reads=[dr["g1"]], writes=[g1row])
    pbc = env.ps("s2_pbc", [128, 512], F32)
    for hh in range(2):
        r.do("pe", "matmul", pbc.ap(), c.ones1.ap(), g1row.ap()[:, 512 * hh:512 * (hh + 1)],
             start=True, stop=True, reads=[c.ones1, g1row], writes=[pbc])
        r.do("sc", "copy", g1bc.ap()[:, 512 * hh:512 * (hh + 1)], pbc.ap(),
             reads=[pbc], writes=[g1bc])
    xch = env.sb("s2_xch", [128, 4, D], F32)
    lnt = env.sb("s2_lnt", [128, D], F32)
    xn = env.sb("s2_xn", [128, D], BF16)
    xn_fm = env.sb("s2_xnfm", [128, 8, 512], BF16)
    qst = env.sb("s2_qst", [128, 512], BF16)
    qst2 = env.sb("s2_qst2", [128, 512], BF16)
    vst = env.sb("s2_vst", [128, 256], BF16)
    ptr = env.ps("s2_ptr", [128, 128], BF16)
    ptr2 = env.ps("s2_ptr2", [128, 128], BF16)
    pQ = env.ps("s2_pQ", [128, 512], F32)
    pQ2 = env.ps("s2_pQ2", [128, 512], F32)
    pV = env.ps("s2_pV", [128, 256], F32)
    Xv = dr["X"].h[:].rearrange("(ch q p) d -> ch q p d", q=4, p=128)
    X2v = dr["X2"].h[:].rearrange("(ch q p) d -> ch q p d", q=4, p=128)
    VCv = dr["VC_loc"].h[:].rearrange("(ch q p) d -> ch q p d", q=4, p=128)
    QLh = dr["QL0"].h[:]
    KCh = dr["KC_loc"].h[:]
    for ch in range(8):
        for qi in range(4):
            r.dma("sy", xch.ap()[:, qi], Xv[ch, qi], reads=[dr["X"]], writes=[xch])
            emit_ln(env, c, xch.ap()[:, qi], xch, lnt.ap(), lnt, c.eps5, True)
            r.dma("gp", X2v[ch, qi], lnt.ap(), reads=[lnt], writes=[dr["X2"]])
            emit_ln(env, c, lnt.ap(), lnt, xn.ap(), xn, c.eps6, False,
                    post=("mulw", g1bc))
            for a in range(8):
                tp = ptr if a % 2 == 0 else ptr2
                r.do("pe", "transpose", tp.ap(), xn.ap()[:, a * 128:(a + 1) * 128],
                     c.id_b.ap(), reads=[xn, c.id_b], writes=[tp])
                r.do("sc", "copy", xn_fm.ap()[:, a, qi * 128:(qi + 1) * 128],
                     tp.ap(), reads=[tp], writes=[xn_fm])
            for a in range(8):
                r.do("pe", "matmul", pV.ap(),
                     xn_fm.ap()[:, a, qi * 128:(qi + 1) * 128],
                     wvc.ap()[:, a, :], start=(a == 0), stop=(a == 7),
                     reads=[xn_fm, wvc], writes=[pV])
            r.do("sc", "copy", vst.ap(), pV.ap(), reads=[pV], writes=[vst])
            r.dma("gp", VCv[ch, qi], vst.ap(), reads=[vst], writes=[dr["VC_loc"]])
        for m in range(8):
            pq, qs = (pQ, qst) if m % 2 == 0 else (pQ2, qst2)
            for a in range(8):
                r.do("pe", "matmul", pq.ap(), wq.ap()[:, a, m * 128:(m + 1) * 128],
                     xn_fm.ap()[:, a, :], start=(a == 0), stop=(a == 7),
                     reads=[wq, xn_fm], writes=[pq])
            r.do("sc", "activation", qs.ap(), pq.ap(), func=AF.Copy, scale=0.0625,
                 reads=[pq], writes=[qs])
            r.dma("gp", QLh[m * 128:(m + 1) * 128, ch * 512:(ch + 1) * 512],
                  qs.ap(), reads=[qs], writes=[dr["QL0"]])
        for m in range(2):
            pq, qs = (pQ, qst) if m % 2 == 0 else (pQ2, qst2)
            for a in range(8):
                r.do("pe", "matmul", pq.ap(), wkc.ap()[:, a, m * 128:(m + 1) * 128],
                     xn_fm.ap()[:, a, :], start=(a == 0), stop=(a == 7),
                     reads=[wkc, xn_fm], writes=[pq])
            r.do("sc", "copy", qs.ap(), pq.ap(), reads=[pq], writes=[qs])
            r.dma("gp", KCh[m * 128:(m + 1) * 128, ch * 512:(ch + 1) * 512],
                  qs.ap(), reads=[qs], writes=[dr["KC_loc"]])





def emit_S4(env, c):
    r = env.r
    dr = env.dram
    kc = env.sb("s4_kc", [128, 2, 4, 4096], BF16)
    vc = env.sb("s4_vc", [128, 128, 256], BF16)
    kcv = dr["KCG"].h[:].rearrange("(rr a p) t -> rr a p t", p=128, a=2)
    for rr in range(4):
        for a in range(2):
            r.dma("sy", kc.ap()[:, a, rr, :], kcv[rr, a],
                  reads=[dr["KCG"]], writes=[kc])
    for part in range(8):
        r.dma("sy", vc.ap()[:, 16 * part:16 * (part + 1), :],
              dr["VCG"].h[:].rearrange("(n p) d -> p n d", p=128)[:, 16 * part:16 * (part + 1), :],
              reads=[dr["VCG"]], writes=[vc])
    wo = env.sb("s4_wo", [128, 8, 1024], BF16)
    r.dma("sy", wo.ap(), dr["WoT0_b"].h[:].rearrange("(a p) f -> p a f", p=128),
          reads=[dr["WoT0_b"]], writes=[wo])
    qh = env.sb("s4_qh", [128, 2, 512], BF16)
    ascr = env.sb("s4_ascr", [128, 1024], BF16)
    mst = env.sb("s4_mst", [128, 4, 16], F32)
    lst = env.sb("s4_lst", [128, 4, 16], F32)
    e16 = env.sb("s4_e16", [128, 16], F32)
    negm8 = env.sb("s4_negm8", [128, 4], F32)
    rl = env.sb("s4_rl", [128, 4], F32)
    negc = env.sb("s4_negc", [128, 1], F32)
    lrow = env.sb("s4_lrow", [1, 512], F32)
    lbc = env.sb("s4_lbc", [128, 512], F32)
    pt1 = env.sb("s4_pt1", [128, 512], BF16)
    pt2 = env.sb("s4_pt2", [128, 512], BF16)
    attn_fm = env.sb("s4_attnfm", [128, 8, 512], BF16)
    xt = env.sb("s4_xt", [128, D], F32)
    xo = env.sb("s4_xo", [128, D], F32)
    SA = env.ps("s4_SA", [128, 1024], F32)
    ST = env.ps("s4_ST", [128, 512], F32)
    pv0 = env.ps("s4_pv0", [128, 512], F32)
    pv1 = env.ps("s4_pv1", [128, 512], F32)
    ptf = env.ps("s4_ptf", [128, 128], F32)
    QLh = dr["QL0"].h[:]
    X2v = dr["X2"].h[:].rearrange("(ch q p) d -> ch q p d", q=4, p=128)
    X2bv = dr["X2b"].h[:].rearrange("(ch q p) d -> ch q p d", q=4, p=128)
    for ch in range(8):
        for h in range(4):
            r.dma("sy", qh.ap(),
                  QLh[h * 256:(h + 1) * 256, ch * 512:(ch + 1) * 512]
                  .rearrange("(a p) t -> p a t", p=128),
                  reads=[dr["QL0"]], writes=[qh])
            for qt in range(4):
                for sup in range(16):
                    rr, tw = sup // 4, sup % 4
                    for half in range(2):
                        for a in range(2):
                            r.do("pe", "matmul", SA.ap()[:, half * 512:(half + 1) * 512],
                                 qh.ap()[:, a, qt * 128:(qt + 1) * 128],
                                 kc.ap()[:, a, rr, tw * 1024 + half * 512: tw * 1024 + (half + 1) * 512],
                                 start=(a == 0), stop=(a == 1),
                                 reads=[qh, kc], writes=[SA])
                    r.do("dv", "reduce_max", mst.ap()[:, qt, sup:sup + 1], SA.ap(),
                         axis=AX.X, negate=True, reads=[SA], writes=[mst])
                    r.do("sc", "activation", ascr.ap(), SA.ap(), func=AF.Exp,
                         bias=mst.ap()[:, qt, sup:sup + 1],
                         accum_out=lst.ap()[:, qt, sup:sup + 1],
                         reads=[SA, mst], writes=[ascr, lst])
                r.do("dv", "tensor_reduce", negm8.ap()[:, qt:qt + 1],
                     mst.ap()[:, qt, :], axis=AX.X, op=ALU.min,
                     reads=[mst], writes=[negm8])
            r.do("pe", "transpose", ptf.ap()[0:4, :], negm8.ap(), c.id_f.ap(),
                 reads=[negm8, c.id_f], writes=[ptf])
            r.do("dv", "tensor_reduce", c.tmp1.ap()[0:4, :], ptf.ap()[0:4, :],
                 axis=AX.X, op=ALU.min, reads=[ptf], writes=[c.tmp1])
            r.do("pe", "transpose", ptf.ap()[0:1, 0:128], c.tmp1.ap()[:, 0:1],
                 c.id_f.ap(), reads=[c.tmp1, c.id_f], writes=[ptf])
            r.do("dv", "tensor_reduce", c.tmp1.ap()[0:1, 0:1], ptf.ap()[0:1, 0:4],
                 axis=AX.X, op=ALU.min, reads=[ptf], writes=[c.tmp1])
            r.do("pe", "matmul", ptf.ap()[:, 0:1], c.ones1.ap(),
                 c.tmp1.ap()[0:1, 0:1], start=True, stop=True,
                 reads=[c.ones1, c.tmp1], writes=[ptf])
            r.do("sc", "copy", negc.ap(), ptf.ap()[:, 0:1], reads=[ptf],
                 writes=[negc])
            sc_accum_fence(env, c, lst)
            for qt in range(4):
                r.do("dv", "tensor_scalar", e16.ap(), mst.ap()[:, qt, :],
                     negc.ap(), None, ALU.subtract, reads=[mst, negc],
                     writes=[e16])
                r.do("sc", "activation", e16.ap(), e16.ap(), func=AF.Exp,
                     scale=-1.0, reads=[e16], writes=[e16])
                r.do("dv", "tensor_mul", e16.ap(), e16.ap(), lst.ap()[:, qt, :],
                     reads=[e16, lst], writes=[e16])
                r.do("dv", "reduce_sum", c.tmp1.ap(), e16.ap(), axis=AX.X,
                     reads=[e16], writes=[c.tmp1])
                r.do("dv", "reciprocal", rl.ap()[:, qt:qt + 1], c.tmp1.ap(),
                     reads=[c.tmp1], writes=[rl])
            for tt in range(128):
                rr, n = tt // 32, tt % 32
                pt = pt1 if tt % 2 == 0 else pt2
                for a in range(2):
                    r.do("pe", "matmul", ST.ap(),
                         kc.ap()[:, a, rr, n * 128:(n + 1) * 128],
                         qh.ap()[:, a, :], start=(a == 0), stop=(a == 1),
                         reads=[kc, qh], writes=[ST])
                r.do("sc", "activation", pt.ap(), ST.ap(), func=AF.Exp,
                     bias=negc.ap(), reads=[ST, negc], writes=[pt])
                for a2 in range(2):
                    pv = pv0 if a2 == 0 else pv1
                    r.do("pe", "matmul", pv.ap(),
                         vc.ap()[:, tt, a2 * 128:(a2 + 1) * 128], pt.ap(),
                         start=(tt == 0), stop=(tt == 127),
                         reads=[vc, pt], writes=[pv])
            for qt in range(4):
                r.do("pe", "transpose", ptf.ap()[0:1, 0:128],
                     rl.ap()[:, qt:qt + 1], c.id_f.ap(),
                     reads=[rl, c.id_f], writes=[ptf])
                r.do("sc", "copy", lrow.ap()[:, qt * 128:(qt + 1) * 128],
                     ptf.ap()[0:1, 0:128], reads=[ptf], writes=[lrow])
            r.do("pe", "matmul", SA.ap()[:, 0:512], c.ones1.ap(), lrow.ap(),
                 start=True, stop=True, reads=[c.ones1, lrow], writes=[SA])
            r.do("sc", "copy", lbc.ap(), SA.ap()[:, 0:512], reads=[SA], writes=[lbc])
            for a2 in range(2):
                pv = pv0 if a2 == 0 else pv1
                r.do("dv", "tensor_tensor", attn_fm.ap()[:, h * 2 + a2, :],
                     pv.ap(), lbc.ap(), ALU.mult, reads=[pv, lbc],
                     writes=[attn_fm])
        for qi in range(4):
            for half in range(2):
                pw = SA if half == 0 else ST
                wslice = pw.ap() if half == 1 else pw.ap()[:, 0:512]
                for a in range(8):
                    r.do("pe", "matmul", wslice,
                         attn_fm.ap()[:, a, qi * 128:(qi + 1) * 128],
                         wo.ap()[:, a, half * 512:(half + 1) * 512],
                         start=(a == 0), stop=(a == 7),
                         reads=[attn_fm, wo], writes=[pw])
            r.dma("sy", xt.ap(), X2v[ch, qi], reads=[dr["X2"]], writes=[xt])
            r.do("dv", "tensor_add", xo.ap()[:, 0:512], SA.ap()[:, 0:512],
                 xt.ap()[:, 0:512], reads=[SA, xt], writes=[xo])
            r.do("dv", "tensor_add", xo.ap()[:, 512:1024], ST.ap(),
                 xt.ap()[:, 512:1024], reads=[ST, xt], writes=[xo])
            r.dma("gp", X2bv[ch, qi], xo.ap(), reads=[xo], writes=[dr["X2b"]])





def emit_top2_gates(env, c, sc_ap, sc_buf, gates_ap, gates_buf, scr):
    r = env.r
    m1, m2, s2t, is1, is2, ec = (scr["m1"], scr["m2"], scr["s2t"], scr["is1"],
                                 scr["is2"], scr["ec"])
    r.do("dv", "reduce_max", m1.ap(), sc_ap, axis=AX.X, reads=[sc_buf], writes=[m1])
    r.do("dv", "tensor_scalar", is1.ap(), sc_ap, m1.ap(), None, ALU.is_ge,
         reads=[sc_buf, m1], writes=[is1])
    r.do("dv", "scalar_tensor_tensor", s2t.ap(), is1.ap(), NEG, sc_ap,
         ALU.mult, ALU.add, reads=[is1, sc_buf], writes=[s2t])
    r.do("dv", "reduce_max", m2.ap(), s2t.ap(), axis=AX.X, reads=[s2t], writes=[m2])
    r.do("dv", "tensor_scalar", is2.ap(), s2t.ap(), m2.ap(), None, ALU.is_ge,
         reads=[s2t, m2], writes=[is2])
    r.do("dv", "tensor_sub", ec.ap(), m2.ap(), m1.ap(), reads=[m1, m2], writes=[ec])
    r.do("sc", "activation", ec.ap(), ec.ap(), func=AF.Exp, reads=[ec], writes=[ec])
    r.do("dv", "tensor_scalar_add", m1.ap(), ec.ap(), 1.0, reads=[ec], writes=[m1])
    r.do("dv", "reciprocal", m1.ap(), m1.ap(), reads=[m1], writes=[m1])
    r.do("dv", "tensor_mul", m2.ap(), ec.ap(), m1.ap(), reads=[ec, m1], writes=[m2])
    r.do("dv", "tensor_scalar", gates_ap, is1.ap(), m1.ap(), None, ALU.mult,
         reads=[is1, m1], writes=[gates_buf])
    r.do("dv", "scalar_tensor_tensor", gates_ap, is2.ap(), m2.ap(), gates_ap,
         ALU.mult, ALU.add, reads=[is2, m2, gates_buf], writes=[gates_buf])


S5_LEVEL = 9


def emit_S5(env, c):
    lvl = float(S5_LEVEL)
    r = env.r
    dr = env.dram
    g2row = env.sb("s5_g2row", [1, D], F32)
    g2bc = env.sb("s5_g2bc", [128, D], F32)
    ebrow = env.sb("s5_ebrow", [1, 8], F32)
    ebbc = env.sb("s5_ebbc", [128, 8], F32)
    wr = env.sb("s5_wr", [128, 8, 8], F32)
    r.dma("sy", g2row.ap(), dr["g2"].h[:], reads=[dr["g2"]], writes=[g2row])
    r.dma("sy", ebrow.ap(), dr["ebias"].h[:], reads=[dr["ebias"]], writes=[ebrow])
    r.dma("sy", wr.ap(), dr["WrT"].h[:].rearrange("(a p) e -> p a e", p=128),
          reads=[dr["WrT"]], writes=[wr])
    ptf = env.ps("s5_ptf", [128, 128], F32)
    pr = env.ps("s5_pr", [128, 8], F32)
    ph1 = env.ps("s5_ph1", [128, 512], F32)
    ph3 = env.ps("s5_ph3", [128, 512], F32)
    py = env.ps("s5_py", [128, 512], F32)
    py2 = env.ps("s5_py2", [128, 512], F32)
    for hh in range(2):
        r.do("pe", "matmul", ph1.ap(), c.ones1.ap(), g2row.ap()[:, 512 * hh:512 * (hh + 1)],
             start=True, stop=True, reads=[c.ones1, g2row], writes=[ph1])
        r.do("sc", "copy", g2bc.ap()[:, 512 * hh:512 * (hh + 1)], ph1.ap(),
             reads=[ph1], writes=[g2bc])
    r.do("pe", "matmul", pr.ap(), c.ones1.ap(), ebrow.ap(), start=True, stop=True,
         reads=[c.ones1, ebrow], writes=[pr])
    r.do("sc", "copy", ebbc.ap(), pr.ap(), reads=[pr], writes=[ebbc])
    x2t = env.sb("s5_x2t", [128, 4, D], F32)
    xf32fm = env.sb("s5_xf32fm", [128, 8, 512], F32)
    xffm = env.sb("s5_xffm", [128, 8, 512], BF16)
    hfm = env.sb("s5_hfm", [128, 8, 512], BF16)
    h1s = env.sb("s5_h1s", [128, 512], F32)
    dacc = env.sb("s5_dacc", [128, 4, D], F32)
    gates = env.sb("s5_gates", [128, 4, 8], F32)
    lg = env.sb("s5_lg", [128, 8], F32)
    w1b = env.sb("s5_w1b", [128, 8, 1024], BF16)
    w3b = env.sb("s5_w3b", [128, 8, 1024], BF16)
    w2h = env.sb("s5_w2h", [128, 8, 512], BF16)
    xo = env.sb("s5_xo", [128, D], F32)
    scr = dict(m1=env.sb("s5_m1", [128, 1], F32), m2=env.sb("s5_m2", [128, 1], F32),
               s2t=env.sb("s5_s2t", [128, 8], F32), is1=env.sb("s5_is1", [128, 8], F32),
               is2=env.sb("s5_is2", [128, 8], F32), ec=env.sb("s5_ec", [128, 1], F32))
    X2bv = dr["X2b"].h[:].rearrange("(ch q p) d -> ch q p d", q=4, p=128)
    Xv = dr["X"].h[:].rearrange("(ch q p) d -> ch q p d", q=4, p=128)
    Xbv = dr["Xb"].h[:].rearrange("(ch q p) d -> ch q p d", q=4, p=128)
    for ch in range(8):
        for qi in range(4):
            r.dma("sy", x2t.ap()[:, qi], X2bv[ch, qi], reads=[dr["X2b"]],
                  writes=[x2t])
            if lvl <= -1:
                continue
            emit_ln(env, c, x2t.ap()[:, qi], x2t, c.scr1.ap(), c.scr1, c.eps6,
                    False, post=("mulw", g2bc))
            if lvl == 0.2:
                r.do("dv", "memset", gates.ap()[:, qi, :], 0.1, writes=[gates])
                continue
            for a in range(8):
                r.do("pe", "transpose", ptf.ap(), c.scr1.ap()[:, a * 128:(a + 1) * 128],
                     c.id_f.ap(), reads=[c.scr1, c.id_f], writes=[ptf])
                r.do("sc", "copy", xf32fm.ap()[:, a, qi * 128:(qi + 1) * 128],
                     ptf.ap(), reads=[ptf], writes=[xf32fm])
                if lvl >= 0.4:
                    r.do("sc", "copy", xffm.ap()[:, a, qi * 128:(qi + 1) * 128],
                         ptf.ap(), reads=[ptf], writes=[xffm])
            if lvl == 0.5 or lvl == 0.4 or lvl == 0.3:
                r.do("dv", "memset", gates.ap()[:, qi, :], 0.1, writes=[gates])
                continue
            for a in range(8):
                r.do("pe", "matmul", pr.ap(),
                     xf32fm.ap()[:, a, qi * 128:(qi + 1) * 128], wr.ap()[:, a, :],
                     start=(a == 0), stop=(a == 7), reads=[xf32fm, wr],
                     writes=[pr])
            r.do("dv", "tensor_add", lg.ap(), pr.ap(), ebbc.ap(),
                 reads=[pr, ebbc], writes=[lg])
            if lvl >= 1:
                r.do("dv", "reduce_max", scr["m1"].ap(), lg.ap(), axis=AX.X,
                     negate=True, reads=[lg], writes=[scr["m1"]])
                r.do("sc", "activation", lg.ap(), lg.ap(), func=AF.Exp,
                     bias=scr["m1"].ap(), accum_out=scr["ec"].ap(),
                     reads=[lg, scr["m1"]], writes=[lg, scr["ec"]])
                r.do("dv", "reciprocal", scr["ec"].ap(), scr["ec"].ap(),
                     reads=[scr["ec"]], writes=[scr["ec"]])
                r.do("dv", "tensor_scalar", lg.ap(), lg.ap(), scr["ec"].ap(), None,
                     ALU.mult, reads=[lg, scr["ec"]], writes=[lg])
                emit_top2_gates(env, c, lg.ap(), lg, gates.ap()[:, qi, :], gates, scr)
            else:
                r.do("dv", "memset", gates.ap()[:, qi, :], 0.1, writes=[gates])
        if lvl < 2 and lvl > -1:
            for qi in range(4):
                r.do("dv", "memset", dacc.ap()[:, qi], 0.0, writes=[dacc])
        for ei in range(9 if lvl >= 3 else (1 if lvl == 2 else 0)):
            if ei == 0:
                w1src, w3src, w2src, row0 = "sW1T_b", "sW3T_b", "sW2T_b", 0
            else:
                w1src, w3src, w2src, row0 = "eW1T_b", "eW3T_b", "eW2T_b", (ei - 1) * 1024
            r.dma("sy", w1b.ap(),
                  dr[w1src].h[:][row0:row0 + 1024, :].rearrange("(a p) f -> p a f", p=128),
                  reads=[dr[w1src]], writes=[w1b])
            r.dma("sy", w3b.ap(),
                  dr[w3src].h[:][row0:row0 + 1024, :].rearrange("(a p) f -> p a f", p=128),
                  reads=[dr[w3src]], writes=[w3b])
            for fc in range(8):
                for a in range(8):
                    r.do("pe", "matmul", ph1.ap(), w1b.ap()[:, a, fc * 128:(fc + 1) * 128],
                         xffm.ap()[:, a, :], start=(a == 0), stop=(a == 7),
                         reads=[w1b, xffm], writes=[ph1])
                for a in range(8):
                    r.do("pe", "matmul", ph3.ap(), w3b.ap()[:, a, fc * 128:(fc + 1) * 128],
                         xffm.ap()[:, a, :], start=(a == 0), stop=(a == 7),
                         reads=[w3b, xffm], writes=[ph3])
                r.do("sc", "activation", h1s.ap(), ph1.ap(), func=AF.Silu,
                     reads=[ph1], writes=[h1s])
                r.do("dv", "tensor_tensor", hfm.ap()[:, fc, :], ph3.ap(), h1s.ap(),
                     ALU.mult, reads=[ph3, h1s], writes=[hfm])
            for half in range(2):
                r.dma("sy", w2h.ap(),
                      dr[w2src].h[:][row0:row0 + 1024, half * 512:(half + 1) * 512]
                      .rearrange("(fc p) d2 -> p fc d2", p=128),
                      reads=[dr[w2src]], writes=[w2h])
                for qi in range(4):
                    pp = py if qi % 2 == 0 else py2
                    if lvl < 3 and False:
                        continue
                    for fc in range(8):
                        r.do("pe", "matmul", pp.ap(),
                             hfm.ap()[:, fc, qi * 128:(qi + 1) * 128],
                             w2h.ap()[:, fc, :], start=(fc == 0), stop=(fc == 7),
                             reads=[hfm, w2h], writes=[pp])
                        if fc == 7:
                            pass
                    dsl = dacc.ap()[:, qi, half * 512:(half + 1) * 512]
                    if ei == 0:
                        r.do("dv", "tensor_copy", dsl, pp.ap(), reads=[pp],
                             writes=[dacc])
                    else:
                        r.do("dv", "scalar_tensor_tensor", dsl, pp.ap(),
                             gates.ap()[:, qi, ei - 1:ei], dsl, ALU.mult, ALU.add,
                             reads=[pp, gates, dacc], writes=[dacc])
        for qi in range(4):
            if lvl <= -1:
                r.do("dv", "memset", dacc.ap()[:, qi], 0.0, writes=[dacc])
            r.do("dv", "tensor_add", c.scr1.ap(), x2t.ap()[:, qi],
                 dacc.ap()[:, qi], reads=[x2t, dacc], writes=[c.scr1])
            emit_ln(env, c, c.scr1.ap(), c.scr1, c.scr2.ap(), c.scr2, c.eps5, True)
            r.dma("sy", xo.ap(), Xv[ch, qi], reads=[dr["X"]], writes=[xo])
            r.do("dv", "tensor_add", xo.ap(), xo.ap(), c.scr2.ap(),
                 reads=[c.scr2, xo], writes=[xo])
            r.dma("gp", Xbv[ch, qi], xo.ap(), reads=[xo], writes=[dr["Xb"]])





S6_LEVEL = 9


def emit_S6(env, c):
    lvl = S6_LEVEL
    r = env.r
    dr = env.dram
    wc = env.sb("s6_wc", [128, 8, 512], BF16)
    r.dma("sy", wc.ap(), dr["Wcat1_b"].h[:].rearrange("(a p) f -> p a f", p=128),
          reads=[dr["Wcat1_b"]], writes=[wc])
    xt = env.sb("s6_xt", [128, D], F32)
    lnb = env.sb("s6_lnb", [128, D], BF16)
    lnfm = env.sb("s6_lnfm", [128, 8, 128], BF16)
    csq = env.sb("s6_csq", [128, 128], F32)
    csk = env.sb("s6_csk", [128, 128], F32)
    qro = env.sb("s6_qro", [128, 256], BF16)
    kro = env.sb("s6_kro", [128, 128], BF16)
    qst = env.sb("s6_qst", [128, 128], BF16)
    kst = env.sb("s6_kst", [128, 128], BF16)
    vst = env.sb("s6_vst", [128, 128], BF16)
    pj = env.ps("s6_pj", [128, 512], F32)
    ptr = env.ps("s6_ptr", [128, 128], BF16)
    ptr2 = env.ps("s6_ptr2", [128, 128], BF16)
    Xbv = dr["Xb"].h[:].rearrange("(n p) d -> n p d", p=128)
    csqv = dr["cs1q"].h[:].rearrange("(n p) d -> n p d", p=128)
    cskv = dr["cs1k"].h[:].rearrange("(n p) d -> n p d", p=128)
    QLh = dr["QL1"].h[:]
    KLh = dr["KL_loc"].h[:]
    VLv = dr["VL_loc"].h[:].rearrange("(n p) d -> n p d", p=128)
    for n in range(32):
        r.dma("sy", xt.ap(), Xbv[n], reads=[dr["Xb"]], writes=[xt])
        r.dma("sy", csq.ap(), csqv[n], reads=[dr["cs1q"]], writes=[csq])
        r.dma("sy", csk.ap(), cskv[n], reads=[dr["cs1k"]], writes=[csk])
        emit_ln(env, c, xt.ap(), xt, lnb.ap(), lnb, c.eps5, True)
        for a in range(8):
            tp = ptr if a % 2 == 0 else ptr2
            r.do("pe", "transpose", tp.ap(), lnb.ap()[:, a * 128:(a + 1) * 128],
                 c.id_b.ap(), reads=[lnb, c.id_b], writes=[tp])
            r.do("sc", "copy", lnfm.ap()[:, a, :], tp.ap(), reads=[tp],
                 writes=[lnfm])
        for a in range(8):
            r.do("pe", "matmul", pj.ap(), lnfm.ap()[:, a, :], wc.ap()[:, a, :],
                 start=(a == 0), stop=(a == 7), reads=[lnfm, wc], writes=[pj])
        r.do("sc", "copy", vst.ap(), pj.ap()[:, 384:512], reads=[pj], writes=[vst])
        r.dma("gp", VLv[n], vst.ap(), reads=[vst], writes=[dr["VL_loc"]])
        if lvl < 3:
            r.do("sc", "copy", qro.ap(), pj.ap()[:, 0:256], reads=[pj], writes=[qro])
            r.do("sc", "copy", kro.ap(), pj.ap()[:, 256:384], reads=[pj], writes=[kro])
        r.do("sc", "copy", c.scr3.ap()[:, 0:384], pj.ap()[:, 0:384],
             reads=[pj], writes=[c.scr3])
        emit_rope(env, c, c.scr3.ap()[:, 0:256], [c.scr3], qro.ap(), qro,
                  csq.ap(), csq, 2, 128)
        emit_rope(env, c, c.scr3.ap()[:, 256:384], [c.scr3], kro.ap(), kro,
                  csk.ap(), csk, 1, 128)
        for h in range(2):
            tp = ptr if h % 2 == 0 else ptr2
            r.do("pe", "transpose", tp.ap(), qro.ap()[:, h * 128:(h + 1) * 128],
                 c.id_b.ap(), reads=[qro, c.id_b], writes=[tp])
            r.do("sc", "copy", qst.ap(), tp.ap(), reads=[tp], writes=[qst])
            r.dma("gp", QLh[h * 128:(h + 1) * 128, n * 128:(n + 1) * 128],
                  qst.ap(), reads=[qst], writes=[dr["QL1"]])
        r.do("pe", "transpose", ptr.ap(), kro.ap(), c.id_b.ap(),
             reads=[kro, c.id_b], writes=[ptr])
        r.do("sc", "copy", kst.ap(), ptr.ap(), reads=[ptr], writes=[kst])
        r.dma("gp", KLh[:, n * 128:(n + 1) * 128], kst.ap(), reads=[kst],
              writes=[dr["KL_loc"]])





def emit_S8(env, c):
    r = env.r
    dr = env.dram
    kg = env.sb("s8_kg", [128, 4, 4096], BF16)
    vg = env.sb("s8_vg", [128, 128, 128], BF16)
    ql = env.sb("s8_ql", [128, 2, 4096], BF16)
    amA = env.sb("s8_amA", [128, 16, 512], F32)
    amB = env.sb("s8_amB", [128, 16, 512], F32)
    wo = env.sb("s8_wo", [128, 2, 1024], BF16)
    kgv = dr["KG"].h[:].rearrange("(rr p) t -> rr p t", p=128)
    for rr in range(4):
        r.dma("sy", kg.ap()[:, rr, :], kgv[rr], reads=[dr["KG"]], writes=[kg])
    for part in range(8):
        r.dma("sy", vg.ap()[:, 16 * part:16 * (part + 1), :],
              dr["VG"].h[:].rearrange("(n p) d -> p n d", p=128)[:, 16 * part:16 * (part + 1), :],
              reads=[dr["VG"]], writes=[vg])
    r.dma("sy", ql.ap(), dr["QL1"].h[:].rearrange("(h p) t -> p h t", p=128),
          reads=[dr["QL1"]], writes=[ql])
    for i in range(16):
        r.dma("sy", amA.ap()[:, i, :], dr["amaskA"].h[:][i], reads=[dr["amaskA"]],
              writes=[amA])
        r.dma("sy", amB.ap()[:, i, :], dr["amaskB"].h[:][i], reads=[dr["amaskB"]],
              writes=[amB])
    r.dma("sy", wo.ap(), dr["WoT1_b"].h[:].rearrange("(a p) f -> p a f", p=128),
          reads=[dr["WoT1_b"]], writes=[wo])
    msk = env.sb("s8_msk", [128, 512], F32)
    ascr = env.sb("s8_ascr", [128, 512], BF16)
    mst = env.sb("s8_mst", [128, 4, 32], F32)
    lst = env.sb("s8_lst", [128, 4, 32], F32)
    e32 = env.sb("s8_e32", [128, 32], F32)
    negm4 = env.sb("s8_negm4", [128, 4], F32)
    rl = env.sb("s8_rl", [128, 4], F32)
    negc = env.sb("s8_negc", [128, 1], F32)
    lrow = env.sb("s8_lrow", [1, 512], F32)
    lbc = env.sb("s8_lbc", [128, 512], F32)
    pt1 = env.sb("s8_pt1", [128, 512], BF16)
    pt2 = env.sb("s8_pt2", [128, 512], BF16)
    attn_fm = env.sb("s8_attnfm", [128, 2, 512], BF16)
    xt = env.sb("s8_xt", [128, D], F32)
    xo = env.sb("s8_xo", [128, D], F32)
    SA = env.ps("s8_SA", [128, 512], F32)
    ST = env.ps("s8_ST", [128, 512], F32)
    pv = env.ps("s8_pv", [128, 512], F32)
    ptf = env.ps("s8_ptf", [128, 128], F32)
    Xbv = dr["Xb"].h[:].rearrange("(s q p) d -> s q p d", q=4, p=128)
    Xcv = dr["Xc"].h[:].rearrange("(s q p) d -> s q p d", q=4, p=128)
    for j in range(8):
        for h in range(2):
            for pi in range(4):
                qsl = ql.ap()[:, h, (j * 4 + pi) * 128:(j * 4 + pi + 1) * 128]
                ns = 0
                for jb in range(j + 1):
                    for rr in range(4):
                        r.do("pe", "matmul", SA.ap(), qsl,
                             kg.ap()[:, rr, jb * 512:(jb + 1) * 512],
                             start=True, stop=True, reads=[ql, kg], writes=[SA])
                        if jb == j:
                            r.do("dv", "tensor_add", msk.ap(), SA.ap(),
                                 amA.ap()[:, rr * 4 + pi, :], reads=[SA, amA],
                                 writes=[msk])
                            srcb, srca = msk, msk.ap()
                        else:
                            srcb, srca = SA, SA.ap()
                        r.do("dv", "reduce_max", mst.ap()[:, pi, ns:ns + 1],
                             srca, axis=AX.X, negate=True, reads=[srcb],
                             writes=[mst])
                        r.do("sc", "activation", ascr.ap(), srca, func=AF.Exp,
                             bias=mst.ap()[:, pi, ns:ns + 1],
                             accum_out=lst.ap()[:, pi, ns:ns + 1],
                             reads=[srcb, mst], writes=[ascr, lst])
                        ns += 1
                r.do("dv", "tensor_reduce", negm4.ap()[:, pi:pi + 1],
                     mst.ap()[:, pi, 0:ns], axis=AX.X, op=ALU.min,
                     reads=[mst], writes=[negm4])
            r.do("pe", "transpose", ptf.ap()[0:4, :], negm4.ap(), c.id_f.ap(),
                 reads=[negm4, c.id_f], writes=[ptf])
            r.do("dv", "tensor_reduce", c.tmp1.ap()[0:4, :], ptf.ap()[0:4, :],
                 axis=AX.X, op=ALU.min, reads=[ptf], writes=[c.tmp1])
            r.do("pe", "transpose", ptf.ap()[0:1, 0:128], c.tmp1.ap()[:, 0:1],
                 c.id_f.ap(), reads=[c.tmp1, c.id_f], writes=[ptf])
            r.do("dv", "tensor_reduce", c.tmp1.ap()[0:1, 0:1], ptf.ap()[0:1, 0:4],
                 axis=AX.X, op=ALU.min, reads=[ptf], writes=[c.tmp1])
            r.do("pe", "matmul", ptf.ap()[:, 0:1], c.ones1.ap(),
                 c.tmp1.ap()[0:1, 0:1], start=True, stop=True,
                 reads=[c.ones1, c.tmp1], writes=[ptf])
            r.do("sc", "copy", negc.ap(), ptf.ap()[:, 0:1], reads=[ptf],
                 writes=[negc])
            ns_all = 4 * (j + 1)
            sc_accum_fence(env, c, lst)
            for pi in range(4):
                r.do("dv", "tensor_scalar", e32.ap()[:, 0:ns_all],
                     mst.ap()[:, pi, 0:ns_all], negc.ap(), None, ALU.subtract,
                     reads=[mst, negc], writes=[e32])
                r.do("sc", "activation", e32.ap()[:, 0:ns_all], e32.ap()[:, 0:ns_all],
                     func=AF.Exp, scale=-1.0, reads=[e32], writes=[e32])
                r.do("dv", "tensor_mul", e32.ap()[:, 0:ns_all], e32.ap()[:, 0:ns_all],
                     lst.ap()[:, pi, 0:ns_all], reads=[e32, lst], writes=[e32])
                r.do("dv", "reduce_sum", c.tmp1.ap(), e32.ap()[:, 0:ns_all],
                     axis=AX.X, reads=[e32], writes=[c.tmp1])
                r.do("dv", "reciprocal", rl.ap()[:, pi:pi + 1], c.tmp1.ap(),
                     reads=[c.tmp1], writes=[rl])
            qseg = ql.ap()[:, h, j * 512:(j + 1) * 512]
            first = True
            n_tt = 16 * (j + 1)
            cnt = 0
            for jb in range(j + 1):
                for rr in range(4):
                    for tp in range(4):
                        pt = pt1 if cnt % 2 == 0 else pt2
                        tloc = jb * 512 + tp * 128
                        r.do("pe", "matmul", ST.ap(),
                             kg.ap()[:, rr, tloc:tloc + 128], qseg,
                             start=True, stop=True, reads=[kg, ql], writes=[ST])
                        if jb == j:
                            r.do("dv", "tensor_add", msk.ap(), ST.ap(),
                                 amB.ap()[:, rr * 4 + tp, :], reads=[ST, amB],
                                 writes=[msk])
                            r.do("sc", "activation", pt.ap(), msk.ap(),
                                 func=AF.Exp, bias=negc.ap(),
                                 reads=[msk, negc], writes=[pt])
                        else:
                            r.do("sc", "activation", pt.ap(), ST.ap(),
                                 func=AF.Exp, bias=negc.ap(),
                                 reads=[ST, negc], writes=[pt])
                        cnt += 1
                        r.do("pe", "matmul", pv.ap(),
                             vg.ap()[:, rr * 32 + jb * 4 + tp, :], pt.ap(),
                             start=first, stop=(cnt == n_tt),
                             reads=[vg, pt], writes=[pv])
                        first = False
            for pi in range(4):
                r.do("pe", "transpose", ptf.ap()[0:1, 0:128],
                     rl.ap()[:, pi:pi + 1], c.id_f.ap(), reads=[rl, c.id_f],
                     writes=[ptf])
                r.do("sc", "copy", lrow.ap()[:, pi * 128:(pi + 1) * 128],
                     ptf.ap()[0:1, 0:128], reads=[ptf], writes=[lrow])
            r.do("pe", "matmul", SA.ap(), c.ones1.ap(), lrow.ap(),
                 start=True, stop=True, reads=[c.ones1, lrow], writes=[SA])
            r.do("sc", "copy", lbc.ap(), SA.ap(), reads=[SA], writes=[lbc])
            r.do("dv", "tensor_tensor", attn_fm.ap()[:, h, :], pv.ap(), lbc.ap(),
                 ALU.mult, reads=[pv, lbc], writes=[attn_fm])
        for qi in range(4):
            for half in range(2):
                pw = SA if half == 0 else ST
                for a2 in range(2):
                    r.do("pe", "matmul", pw.ap(),
                         attn_fm.ap()[:, a2, qi * 128:(qi + 1) * 128],
                         wo.ap()[:, a2, half * 512:(half + 1) * 512],
                         start=(a2 == 0), stop=(a2 == 1),
                         reads=[attn_fm, wo], writes=[pw])
            r.do("dv", "tensor_copy", c.scr1.ap()[:, 0:512], SA.ap(),
                 reads=[SA], writes=[c.scr1])
            r.do("dv", "tensor_copy", c.scr1.ap()[:, 512:1024], ST.ap(),
                 reads=[ST], writes=[c.scr1])
            emit_ln(env, c, c.scr1.ap(), c.scr1, c.scr2.ap(), c.scr2, c.eps5, True)
            r.dma("sy", xt.ap(), Xbv[j, qi], reads=[dr["Xb"]], writes=[xt])
            r.do("dv", "tensor_add", xo.ap(), xt.ap(), c.scr2.ap(),
                 reads=[xt, c.scr2], writes=[xo])
            r.dma("gp", Xcv[j, qi], xo.ap(), reads=[xo], writes=[dr["Xc"]])





def emit_S9(env, c):
    r = env.r
    dr = env.dram
    rw = env.sb("s9_rw", [128, 8, 8], F32)
    rbrow = env.sb("s9_rbrow", [1, 8], F32)
    rbbc = env.sb("s9_rbbc", [128, 8], F32)
    lngsb = env.sb("s9_lng", [128, 8, 8], F32)
    lnbsb = env.sb("s9_lnb", [128, 8, 8], F32)
    b1sb = env.sb("s9_b1", [128, 32, 8], F32)
    b2sb = env.sb("s9_b2", [8, 1024], F32)
    r.dma("sy", rw.ap(), dr["rW"].h[:].rearrange("(a p) e -> p a e", p=128),
          reads=[dr["rW"]], writes=[rw])
    r.dma("sy", rbrow.ap(), dr["rb"].h[:], reads=[dr["rb"]], writes=[rbrow])
    r.dma("sy", lngsb.ap(), dr["ln_gT"].h[:].rearrange("(a p) e -> p a e", p=128),
          reads=[dr["ln_gT"]], writes=[lngsb])
    r.dma("sy", lnbsb.ap(), dr["ln_bT"].h[:].rearrange("(a p) e -> p a e", p=128),
          reads=[dr["ln_bT"]], writes=[lnbsb])
    r.dma("sy", b1sb.ap(), dr["b1T"].h[:].rearrange("(fc p) e -> p fc e", p=128),
          reads=[dr["b1T"]], writes=[b1sb])
    r.dma("sy", b2sb.ap(), dr["b2_1"].h[:], reads=[dr["b2_1"]], writes=[b2sb])
    ptf = env.ps("s9_ptf", [128, 128], F32)
    ptb = env.ps("s9_ptb", [128, 128], BF16)
    pa = env.ps("s9_pa", [128, 1024], F32)
    py = env.ps("s9_py", [128, 512], F32)
    py2 = env.ps("s9_py2", [128, 512], F32)
    r.do("pe", "matmul", ptf.ap()[:, 0:8], c.ones1.ap(), rbrow.ap(),
         start=True, stop=True, reads=[c.ones1, rbrow], writes=[ptf])
    r.do("sc", "copy", rbbc.ap(), ptf.ap()[:, 0:8], reads=[ptf], writes=[rbbc])
    xt = env.sb("s9_xt", [128, D], F32)
    linfm = env.sb("s9_linfm", [128, 8, 128], F32)
    xhb = env.sb("s9_xhb", [128, D], BF16)
    xhfm = env.sb("s9_xhfm", [128, 8, 1024], BF16)
    hef = env.sb("s9_hef", [128, 8, 1024], BF16)
    a_sb = env.sb("s9_a", [128, 32, 1024], BF16)
    dacc = env.sb("s9_dacc", [128, 8, D], F32)
    gates = env.sb("s9_gates", [128, 8, 8], F32)
    sumg = env.sb("s9_sumg", [128, 8], F32)
    gT = env.sb("s9_gT", [8, 128], F32)
    lgt = env.sb("s9_lgt", [128, 8], F32)
    w1sl = env.sb("s9_w1sl", [128, 8, 512], BF16)
    w1sl2 = env.sb("s9_w1sl2", [128, 8, 512], BF16)
    w2h = env.sb("s9_w2h", [128, 32, 256], BF16)
    xo = env.sb("s9_xo", [128, D], F32)
    scr = dict(m1=env.sb("s9_m1", [128, 1], F32), m2=env.sb("s9_m2", [128, 1], F32),
               s2t=env.sb("s9_s2t", [128, 8], F32), is1=env.sb("s9_is1", [128, 8], F32),
               is2=env.sb("s9_is2", [128, 8], F32), ec=env.sb("s9_ec", [128, 1], F32))
    Xcv = dr["Xc"].h[:].rearrange("(ch q p) d -> ch q p d", q=8, p=128)
    Yv = dr["Y"].h[:].rearrange("(ch q p) d -> ch q p d", q=8, p=128)
    W1b = dr["W1_1_b"].h[:]
    W2b = dr["W2_1_b"].h[:]
    for ch in range(4):
        for qi in range(8):
            r.dma("sy", xt.ap(), Xcv[ch, qi], reads=[dr["Xc"]], writes=[xt])
            emit_ln(env, c, xt.ap(), xt, c.scr1.ap(), c.scr1, c.eps5, True)
            for a in range(8):
                r.do("pe", "transpose", ptf.ap(), c.scr1.ap()[:, a * 128:(a + 1) * 128],
                     c.id_f.ap(), reads=[c.scr1, c.id_f], writes=[ptf])
                r.do("sc", "copy", linfm.ap()[:, a, :], ptf.ap(), reads=[ptf],
                     writes=[linfm])
            for a in range(8):
                r.do("pe", "matmul", ptf.ap()[:, 0:8], linfm.ap()[:, a, :],
                     rw.ap()[:, a, :], start=(a == 0), stop=(a == 7),
                     reads=[linfm, rw], writes=[ptf])
            r.do("dv", "tensor_add", lgt.ap(), ptf.ap()[:, 0:8], rbbc.ap(),
                 reads=[ptf, rbbc], writes=[lgt])
            emit_top2_gates(env, c, lgt.ap(), lgt, gates.ap()[:, qi, :], gates, scr)
            r.do("dv", "reduce_sum", sumg.ap()[:, qi:qi + 1], gates.ap()[:, qi, :],
                 axis=AX.X, reads=[gates], writes=[sumg])
            emit_ln(env, c, c.scr1.ap(), c.scr1, xhb.ap(), xhb, c.eps5, True)
            for a in range(8):
                r.do("pe", "transpose", ptb.ap(),
                     xhb.ap()[:, a * 128:(a + 1) * 128], c.id_b.ap(),
                     reads=[xhb, c.id_b], writes=[ptb])
                r.do("sc", "copy", xhfm.ap()[:, a, qi * 128:(qi + 1) * 128],
                     ptb.ap(), reads=[ptb], writes=[xhfm])
        for e in range(8):
            for a in range(8):
                r.do("dv", "tensor_scalar", hef.ap()[:, a, :], xhfm.ap()[:, a, :],
                     lngsb.ap()[:, a, e:e + 1], lnbsb.ap()[:, a, e:e + 1],
                     ALU.mult, ALU.add, reads=[xhfm, lngsb, lnbsb], writes=[hef])
            for fq in range(8):
                wsl = w1sl if fq % 2 == 0 else w1sl2
                r.dma("sy", wsl.ap(),
                      W1b[e * 1024:(e + 1) * 1024, fq * 512:(fq + 1) * 512]
                      .rearrange("(a p) f -> p a f", p=128),
                      reads=[dr["W1_1_b"]], writes=[wsl])
                for fi in range(4):
                    fc = fq * 4 + fi
                    for half in range(2):
                        for a in range(8):
                            r.do("pe", "matmul",
                                 pa.ap()[:, half * 512:(half + 1) * 512],
                                 wsl.ap()[:, a, fi * 128:(fi + 1) * 128],
                                 hef.ap()[:, a, half * 512:(half + 1) * 512],
                                 start=(a == 0), stop=(a == 7),
                                 reads=[wsl, hef], writes=[pa])
                    r.do("sc", "activation", a_sb.ap()[:, fc, :], pa.ap(),
                         func=AF.Gelu, bias=b1sb.ap()[:, fc, e:e + 1],
                         reads=[pa, b1sb], writes=[a_sb])
            for quarter in range(4):
                r.dma("sy", w2h.ap(),
                      W2b[e * 4096:(e + 1) * 4096, quarter * 256:(quarter + 1) * 256]
                      .rearrange("(fc p) d2 -> p fc d2", p=128),
                      reads=[dr["W2_1_b"]], writes=[w2h])
                for qi in range(8):
                    pp = py if qi % 2 == 0 else py2
                    for fc in range(32):
                        r.do("pe", "matmul", pp.ap()[:, 0:256],
                             a_sb.ap()[:, fc, qi * 128:(qi + 1) * 128],
                             w2h.ap()[:, fc, :], start=(fc == 0), stop=(fc == 31),
                             reads=[a_sb, w2h], writes=[pp])
                    dsl = dacc.ap()[:, qi, quarter * 256:(quarter + 1) * 256]
                    if e == 0:
                        r.do("dv", "tensor_scalar", dsl, pp.ap()[:, 0:256],
                             gates.ap()[:, qi, 0:1], None, ALU.mult,
                             reads=[pp, gates], writes=[dacc])
                    else:
                        r.do("dv", "scalar_tensor_tensor", dsl, pp.ap()[:, 0:256],
                             gates.ap()[:, qi, e:e + 1], dsl, ALU.mult, ALU.add,
                             reads=[pp, gates, dacc], writes=[dacc])
        for qi in range(8):
            r.dma("sy", xt.ap(), Xcv[ch, qi], reads=[dr["Xc"]], writes=[xt])
            emit_ln(env, c, xt.ap(), xt, c.scr1.ap(), c.scr1, c.eps5, True)
            r.do("pe", "transpose", ptf.ap()[0:8, :], gates.ap()[:, qi, :],
                 c.id_f.ap(), reads=[gates, c.id_f], writes=[ptf])
            r.do("sc", "copy", gT.ap(), ptf.ap()[0:8, :], reads=[ptf], writes=[gT])
            for half in range(2):
                r.do("pe", "matmul", pa.ap()[:, half * 512:(half + 1) * 512],
                     gT.ap(), b2sb.ap()[:, half * 512:(half + 1) * 512],
                     start=True, stop=True, reads=[gT, b2sb], writes=[pa])
            r.do("dv", "tensor_add", c.scr2.ap(), dacc.ap()[:, qi], pa.ap(),
                 reads=[dacc, pa], writes=[c.scr2])
            r.do("dv", "scalar_tensor_tensor", c.scr2.ap(), c.scr1.ap(),
                 sumg.ap()[:, qi:qi + 1], c.scr2.ap(), ALU.mult, ALU.add,
                 reads=[c.scr1, sumg, c.scr2], writes=[c.scr2])
            emit_ln(env, c, c.scr2.ap(), c.scr2, c.scr3.ap(), c.scr3, c.eps5, True)
            r.do("dv", "tensor_add", xo.ap(), xt.ap(), c.scr3.ap(),
                 reads=[xt, c.scr3], writes=[xo])
            r.dma("gp", Yv[ch, qi], xo.ap(), reads=[xo], writes=[dr["Y"]])


EMITTERS = {"S1": emit_S1, "S2": emit_S2, "S4": emit_S4, "S5": emit_S5,
            "S6": emit_S6, "S8": emit_S8, "S9": emit_S9}


STAGE_DEFS = {
    "S1": dict(ins=[], outs=["X"], weights=["w_qkv", "w_out"],
               smalls=["b_out", "norm_w", "pm_kT", "pm_v", "cs0"], need_x0=True),
    "S2": dict(ins=["X"], outs=["X2", "QL0", "KC_loc", "VC_loc"],
               weights=["WqT", "WkcT", "WvcT"], smalls=["g1"]),
    "S4": dict(ins=["X2", "QL0", "KCG", "VCG"], outs=["X2b"],
               weights=["WoT0"], smalls=[]),
    "S5": dict(ins=["X", "X2b"], outs=["Xb"],
               weights=["sW1T", "sW3T", "sW2T", "eW1T", "eW3T", "eW2T"],
               smalls=["g2", "WrT", "ebias"]),
    "S6": dict(ins=["Xb"], outs=["QL1", "KL_loc", "VL_loc"],
               weights=["Wcat1"], smalls=["cs1q", "cs1k"]),
    "S8": dict(ins=["Xb", "QL1", "KG", "VG"], outs=["Xc"],
               weights=["WoT1"], smalls=["amaskA", "amaskB"]),
    "S9": dict(ins=["Xc"], outs=[], weights=["W1_1", "W2_1"],
               smalls=["rW", "rb", "ln_gT", "ln_bT", "b1T", "b2_1"], final=True),
}



def emit_gather(env, *names):
    raise NotImplementedError


def build_masks(H):
    amA, amB = [], []
    for core in range(8):
        rk = core % 4
        a = np.zeros((16, 128, 512), np.float32)
        b = np.zeros((16, 128, 512), np.float32)
        for rr in range(4):
            for pi in range(4):
                qq = pi * 128 + np.arange(128)[:, None]
                tt = np.arange(512)[None, :]
                if rr < rk:
                    m = np.zeros((128, 512), np.float32)
                elif rr > rk:
                    m = np.full((128, 512), NEG, np.float32)
                else:
                    m = np.where(tt <= qq, 0.0, NEG).astype(np.float32)
                a[rr * 4 + pi] = m
            for tp in range(4):
                tt = tp * 128 + np.arange(128)[:, None]
                qq = np.arange(512)[None, :]
                if rr < rk:
                    m = np.zeros((128, 512), np.float32)
                elif rr > rk:
                    m = np.full((128, 512), NEG, np.float32)
                else:
                    m = np.where(tt <= qq, 0.0, NEG).astype(np.float32)
                b[rr * 4 + tp] = m
        amA.append(a)
        amB.append(b)
    H["amaskA"] = amA
    H["amaskB"] = amB


PARANOID_STAGES = {"S1", "S2", "S4", "S5", "S6", "S8", "S9"}


def build_stage(stage):
    global PARANOID
    sd = STAGE_DEFS[stage]
    nc = bass.Bass("TRN2", target_bir_lowering=False)
    es = ExitStack()
    env = Env(nc, es)
    declare_io(env, ins=sd["ins"], outs=sd["outs"], weights=sd["weights"],
               smalls=sd["smalls"], final=sd.get("final", False),
               need_x0=sd.get("need_x0", False))
    consts = {}
    c = setup_common(env, consts)
    env.begin_stage()
    emit_S0(env, sd["weights"])
    env.end_stage()
    env.begin_stage()
    PARANOID = stage in PARANOID_STAGES
    EMITTERS[stage](env, c)
    PARANOID = False
    env.end_stage()
    env.r.finale()
    es.close()
    return nc, env


GROUPS = {
    "A": ["S1", "S2"],
    "B": ["S4", "S5", "S6"],
    "C": ["S8", "S9"],
}
GATHERED = {"KCG", "VCG", "KG", "VG"}


def build_group(stages):
    nc = bass.Bass("TRN2", target_bir_lowering=False)
    es = ExitStack()
    env = Env(nc, es)
    produced, consumed, weights, smalls, finals = set(), set(), [], [], False
    need_x0 = False
    for st in stages:
        sd = STAGE_DEFS[st]
        for nm in sd["ins"]:
            if nm not in produced:
                consumed.add(nm)
        produced.update(sd["outs"])
        weights += [w for w in sd["weights"] if w not in weights]
        smalls += [s for s in sd["smalls"] if s not in smalls]
        finals = finals or sd.get("final", False)
        need_x0 = need_x0 or sd.get("need_x0", False)
    downstream = set()
    seen = False
    order = ["A", "B", "C"]
    gi = [k for k in order if GROUPS[k] == stages][0]
    for k in order[order.index(gi) + 1:]:
        for st in GROUPS[k]:
            downstream.update(STAGE_DEFS[st]["ins"])
    ext_out = {nm for nm in produced
               if nm in downstream or any(
                   nm.startswith(p) for p in ("KC_loc", "VC_loc", "KL_loc", "VL_loc"))}
    internal = produced - ext_out
    declare_io(env, ins=sorted(consumed), outs=sorted(ext_out), weights=weights,
               smalls=smalls, final=finals, need_x0=need_x0)
    hs = {nm: (sh, dt) for (nm, sh, dt) in HANDOFF}
    for nm in sorted(internal):
        sh, dt = hs[nm]
        env.dram_decl(nm, sh, dt, "Internal")
    c = setup_common(env, {})
    env.begin_stage()
    emit_S0(env, weights)
    env.end_stage()
    global PARANOID
    for st in stages:
        env.begin_stage()
        PARANOID = st in PARANOID_STAGES
        EMITTERS[st](env, c)
        PARANOID = False
        env.end_stage()
    env.r.finale()
    es.close()
    return nc, env


def group_in_map(stages, H, core, inter, env):
    m = {}
    for nm, b in env.dram.items():
        pass
    sds = [STAGE_DEFS[st] for st in stages]
    need = set()
    if any(sd.get("need_x0") for sd in sds):
        m["x0"] = H["x0"][core]
    wts, sms = [], []
    for sd in sds:
        wts += sd["weights"]
        sms += sd["smalls"]
    for nm in set(wts):
        m[nm] = H[nm].reshape(dict(WEIGHT_SPECS)[nm])
    for nm in set(sms):
        v = H[nm]
        m[nm] = v[core] if isinstance(v, list) else v
    produced = set()
    for sd in sds:
        for nm in sd["ins"]:
            if nm not in produced:
                m[nm] = inter[nm][core]
        produced.update(sd["outs"])
    return m


def build_fused():
    nc = bass.Bass("TRN2", target_bir_lowering=False)
    es = ExitStack()
    env = Env(nc, es)
    declare_io_fused(env)
    c = setup_common(env, {})
    order = ["S1", "S2", "G0", "S4", "S5", "S6", "G1", "S8", "S9"]
    for st in order:
        env.begin_stage()
        if st == "G0":
            emit_gather(env, "KC_loc", "KCG", "VC_loc", "VCG")
        elif st == "G1":
            emit_gather(env, "KL_loc", "KG", "VL_loc", "VG")
        else:
            emit_S0(env, STAGE_DEFS[st]["weights"])
            EMITTERS[st](env, c)
        env.end_stage()
    env.r.finale()
    es.close()
    return nc, env


def stage_in_map(stage, H, core, inter):
    sd = STAGE_DEFS[stage]
    m = {}
    if sd.get("need_x0"):
        m["x0"] = H["x0"][core]
    for nm in sd["weights"]:
        m[nm] = H[nm].reshape(dict(WEIGHT_SPECS)[nm])
    for nm in sd["smalls"]:
        v = H[nm]
        if isinstance(v, list):
            v = v[core]
        m[nm] = v
    for nm in sd["ins"]:
        m[nm] = inter[nm][core]
    return m


def kernel(**inputs):
    H = host_prep(inputs)
    build_masks(H)
    inter = {}
    if FUSED:
        nc, env = build_fused()
        in_maps = []
        for cpu in range(8):
            m = {"x0": H["x0"][cpu]}
            for (nm, sh) in WEIGHT_SPECS:
                m[nm] = H[nm].reshape(sh)
            for nm, sh in SMALL_INPUTS:
                v = H[nm]
                m[nm] = v[cpu] if isinstance(v, list) else v
            in_maps.append(m)
        res = run_bass_kernel_spmd(nc, in_maps, core_ids=list(range(8)))
        shards = [res.results[cpu]["Y"] for cpu in range(8)]
    elif GROUPED:
        for g in ["A", "G0", "B", "G1", "C"]:
            if g == "G0":
                for (loc, gl) in (("KC_loc", "KCG"), ("VC_loc", "VCG")):
                    inter[gl] = [np.ascontiguousarray(np.concatenate(
                        [inter[loc][(cpu // 4) * 4 + rr] for rr in range(4)], axis=0))
                        for cpu in range(8)]
                continue
            if g == "G1":
                for (loc, gl) in (("KL_loc", "KG"), ("VL_loc", "VG")):
                    inter[gl] = [np.ascontiguousarray(np.concatenate(
                        [inter[loc][(cpu // 4) * 4 + rr] for rr in range(4)], axis=0))
                        for cpu in range(8)]
                continue
            stages = GROUPS[g]
            nc, env = build_group(stages)
            in_maps = [group_in_map(stages, H, cpu, inter, env) for cpu in range(8)]
            res = run_bass_kernel_spmd(nc, in_maps, core_ids=list(range(8)))
            outnames = [nm for nm, b in env.dram.items()
                        if b.h.kind == "ExternalOutput"] if False else None
            for st in stages:
                sd = STAGE_DEFS[st]
                for nm in list(sd["outs"]) + (["Y"] if sd.get("final") else []):
                    if nm in res.results[0]:
                        inter[nm] = [res.results[cpu][nm] for cpu in range(8)]
        shards = inter["Y"]
    else:
        for st in ["S1", "S2", "G0", "S4", "S5", "S6", "G1", "S8", "S9"]:
            if st == "G0":
                for (loc, gl, axis) in (("KC_loc", "KCG", 0), ("VC_loc", "VCG", 0)):
                    inter[gl] = [np.concatenate(
                        [inter[loc][(cpu // 4) * 4 + rr] for rr in range(4)], axis=0)
                        for cpu in range(8)]
                continue
            if st == "G1":
                for (loc, gl) in (("KL_loc", "KG"), ("VL_loc", "VG")):
                    inter[gl] = [np.concatenate(
                        [inter[loc][(cpu // 4) * 4 + rr] for rr in range(4)], axis=0)
                        for cpu in range(8)]
                continue
            nc, env = build_stage(st)
            in_maps = [stage_in_map(st, H, cpu, inter) for cpu in range(8)]
            res = run_bass_kernel_spmd(nc, in_maps, core_ids=list(range(8)))
            sd = STAGE_DEFS[st]
            outs = list(sd["outs"]) + (["Y"] if sd.get("final") else [])
            for nm in outs:
                inter[nm] = [res.results[cpu][nm] for cpu in range(8)]
        shards = inter["Y"]
    out = np.zeros((2, 16384, D), np.float32)
    for cpu in range(8):
        b, rr = cpu // 4, cpu % 4
        sh = shards[cpu]
        for j in range(8):
            out[b, (rr + 4 * j) * SEG:(rr + 4 * j + 1) * SEG] = \
                sh[j * SEG:(j + 1) * SEG]
    return out
